# version 10
# baseline (speedup 1.0000x reference)
import numpy as np
import ml_dtypes
from contextlib import ExitStack

import concourse.bass as bass
import concourse.mybir as mybir
from concourse.bass_utils import run_bass_kernel_spmd

F32 = mybir.dt.float32
BF16 = mybir.dt.bfloat16
U32 = mybir.dt.uint32
AF = mybir.ActivationFunctionType
ALU = mybir.AluOpType
AX = mybir.AxisListType

D = 2048
S = 2048
NSEQ = 2
GRP = 1024
NG = S // GRP
NTC = GRP // 128
KD = D // 128
IC = 16
NEXP = 9
ECAP = 1024
NTOK = NSEQ * S
BIGIDX = 1 << 20
EPS = 1e-5

ENGINES = ("pe", "dve", "act", "gp", "sync")
WLANES = ("W0", "W1", "W2", "W3", "W4", "W5")
XLANES = ("X0", "X1", "X2", "X3")
SLANES = ("S0", "S1")
TLANES = ("T0", "T1")
SEMKEYS = WLANES + XLANES + SLANES + TLANES + ("P", "V", "A")


class Plan:
    def __init__(self):
        self.ops = {e: [] for e in ENGINES}
        self.c = {k: 0 for k in SEMKEYS}

    def op(self, eng, body=None, waits=(), inc=None, dma=False):
        waits = tuple((k, v) for (k, v) in waits if v is not None and v > 0)
        amt = 16 if dma else 1
        val = None
        if inc is not None:
            self.c[inc] += amt
            val = self.c[inc]
        self.ops[eng].append((body, waits, inc, val, amt))
        return val

    def selfwait(self, eng):
        own = {"pe": "P", "dve": "V", "act": "A"}[eng]
        self.op(eng, None, waits=[(own, self.c[own])])

    def run(self, eng, E, sems):
        for body, waits, inc, val, amt in self.ops[eng]:
            for (k, v) in waits:
                E.wait_ge(sems[k], v)
            ins = body(E) if body is not None else None
            if inc is not None:
                assert ins is not None, f"inc {inc} on bodyless op ({eng})"
                ins.then_inc(sems[inc], amt)

    def simulate(self):
        idx = {e: 0 for e in ENGINES}
        cnt = {k: 0 for k in SEMKEYS}
        progress = True
        while progress:
            progress = False
            for e in ENGINES:
                while idx[e] < len(self.ops[e]):
                    body, waits, inc, val, amt = self.ops[e][idx[e]]
                    if all(cnt[k] >= v for (k, v) in waits):
                        if inc is not None:
                            cnt[inc] += amt
                        idx[e] += 1
                        progress = True
                    else:
                        break
        stuck = {}
        for e in ENGINES:
            if idx[e] < len(self.ops[e]):
                body, waits, inc, val, amt = self.ops[e][idx[e]]
                stuck[e] = (idx[e], len(self.ops[e]),
                            [(k, v, cnt[k]) for (k, v) in waits])
        if stuck:
            raise RuntimeError(f"protocol deadlock: {stuck}")
        for k in SEMKEYS:
            assert cnt[k] == self.c[k], (k, cnt[k], self.c[k])


def build_program(do_attn=True, do_ffn=True, dump_xn=False,
                  n_seq=NSEQ, n_g=NG, p2_extras=True, skip_sweep2=False,
                  dump_dbg=False):
    nc = bass.Bass("TRN2", target_bir_lowering=False, detect_race_conditions=True)

    x_d = nc.dram_tensor("x", [NSEQ, S, D], F32, kind="ExternalInput")
    wq_d = nc.dram_tensor("wq", [D, D], F32, kind="ExternalInput")
    wk_d = nc.dram_tensor("wk", [D, D], F32, kind="ExternalInput")
    wv_d = nc.dram_tensor("wv", [D, D], F32, kind="ExternalInput")
    wo_d = nc.dram_tensor("wo", [D, D], F32, kind="ExternalInput")
    bo_d = nc.dram_tensor("bo", [1, D], F32, kind="ExternalInput")
    rt_d = nc.dram_tensor("router", [D, 8], F32, kind="ExternalInput")
    gw_d = nc.dram_tensor("gw", [8, D, D], F32, kind="ExternalInput")
    uw_d = nc.dram_tensor("uw", [8, D, D], F32, kind="ExternalInput")
    dw_d = nc.dram_tensor("dw", [8, D, D], F32, kind="ExternalInput")
    sg_d = nc.dram_tensor("sgw", [D, D], F32, kind="ExternalInput")
    su_d = nc.dram_tensor("suw", [D, D], F32, kind="ExternalInput")
    sd_d = nc.dram_tensor("sdw", [D, D], F32, kind="ExternalInput")
    m01_d = nc.dram_tensor("m01", [128, 128], F32, kind="ExternalInput")
    iota_d = nc.dram_tensor("iota", [128, NTOK // 128], U32, kind="ExternalInput")
    eoff_d = nc.dram_tensor("eoff", [8, 1], F32, kind="ExternalInput")
    xn_sd = nc.dram_tensor("xnsd", [NTOK, D], BF16, kind="Internal")
    sc_sd = nc.dram_tensor("scsd", [NTOK, 1], F32, kind="Internal")
    rank_sd = nc.dram_tensor("ranksd", [1, NTOK], U32, kind="Internal")
    gl_sd = nc.dram_tensor("glsd", [8 * ECAP, 1], U32, kind="Internal")
    idb_d = nc.dram_tensor("idb", [128, 128], BF16, kind="ExternalInput")
    out_d = nc.dram_tensor("out", [NSEQ, S, D], F32, kind="ExternalOutput")
    if dump_xn:
        xnd_d = nc.dram_tensor("xnd", [NSEQ, NG, 128, KD * GRP], BF16,
                               kind="ExternalOutput")
    if dump_dbg:
        dbg_hb = nc.dram_tensor("dbg_hb", [128, 8192], BF16, kind="ExternalOutput")
        dbg_kvb = nc.dram_tensor("dbg_kvb", [128, IC * 128], BF16, kind="ExternalOutput")
        dbg_tm = nc.dram_tensor("dbg_tm", [128, IC * 128], BF16, kind="ExternalOutput")
        dbg_kv = nc.dram_tensor("dbg_kv", [128, IC * 129], F32, kind="ExternalOutput")
        dbg_qb = nc.dram_tensor("dbg_qb", [128, 1024], BF16, kind="ExternalOutput")
        dbg_acc = nc.dram_tensor("dbg_acc", [128, NTC * D], F32, kind="ExternalOutput")
        dbg_meta = nc.dram_tensor("dbg_meta", [128, 256], F32, kind="ExternalOutput")
        dbg_xn2 = nc.dram_tensor("dbg_xn2", [128, KD * GRP], BF16, kind="ExternalOutput")
        dbg_meta2 = nc.dram_tensor("dbg_meta2", [128, 256], F32, kind="ExternalOutput")
        dbg_srep = nc.dram_tensor("dbg_srep", [128, 1024], F32, kind="ExternalOutput")
        dbg_scol = nc.dram_tensor("dbg_scol", [128, 8], F32, kind="ExternalOutput")
        dbg_gls = nc.dram_tensor("dbg_gls", [128, 8], U32, kind="ExternalOutput")

    P = Plan()

    with ExitStack() as ctx:
        e_ = ctx.enter_context
        XN = e_(nc.sbuf_tensor("XN", [128, KD * GRP], BF16))
        ACC = e_(nc.sbuf_tensor("ACC", [128, NTC * D], F32))
        WA = e_(nc.sbuf_tensor("WA", [128, 24576], BF16))
        HB = e_(nc.sbuf_tensor("HB", [128, 8192], BF16))
        XF = e_(nc.sbuf_tensor("XF", [128, D], F32))
        KV = e_(nc.sbuf_tensor("KV", [128, IC * 129], F32))
        KVB = e_(nc.sbuf_tensor("KVB", [128, IC * 128], BF16))
        TMASK = e_(nc.sbuf_tensor("TMASK", [128, IC * 128], BF16))
        QB = e_(nc.sbuf_tensor("QB", [128, 1024], BF16))
        DIAG = e_(nc.sbuf_tensor("DIAG", [128, 128], BF16))
        META = e_(nc.sbuf_tensor("META", [128, 256], F32))
        RC = e_(nc.sbuf_tensor("RC", [128, 512], F32))
        RR = e_(nc.sbuf_tensor("RR", [128, 512], F32))
        SREP = e_(nc.sbuf_tensor("SREP", [128, 1024], F32))
        LG = e_(nc.sbuf_tensor("LG", [128, 8], F32))
        MX8 = e_(nc.sbuf_tensor("MX8", [128, 8], F32))
        IDX8 = e_(nc.sbuf_tensor("IDX8", [128, 8], U32))
        IDXG = e_(nc.sbuf_tensor("IDXG", [128, 8], U32))
        SC8B = e_(nc.sbuf_tensor("SC8B", [128, 8], BF16))
        M01F = e_(nc.sbuf_tensor("M01F", [128, 128], F32))
        IDB = e_(nc.sbuf_tensor("IDB", [128, 128], BF16))
        RB = e_(nc.sbuf_tensor("RB", [128, KD * 8], BF16))
        BO = e_(nc.sbuf_tensor("BO", [1, D], BF16))
        ONEB = e_(nc.sbuf_tensor("ONEB", [1, 128], BF16))
        EPSC = e_(nc.sbuf_tensor("EPSC", [128, 1], F32))
        MASKB = e_(nc.sbuf_tensor("MASKB", [128, 8], BF16))
        RKU = e_(nc.sbuf_tensor("RKU", [1, 1024], U32))
        ONES8F = e_(nc.sbuf_tensor("ONES8F", [8, 1], F32))
        IOTA = e_(nc.sbuf_tensor("IOTA", [128, NTOK // 128], U32))
        GLS = e_(nc.sbuf_tensor("GLS", [128, 8], U32))
        SCOLS = e_(nc.sbuf_tensor("SCOLS", [128, 8], F32))
        SCB = e_(nc.sbuf_tensor("SCB", [128, 8], BF16))
        RCOL = e_(nc.sbuf_tensor("RCOL", [128, 2], U32))
        CARRY = e_(nc.sbuf_tensor("CARRY", [8, 1], F32))
        EOFF = e_(nc.sbuf_tensor("EOFF", [8, 1], F32))
        PB = [e_(nc.psum_tensor(f"PB{i}", [128, 512], F32)) for i in range(8)]

        sems = {k: e_(nc.semaphore(f"sem{k}")) for k in SEMKEYS}

        M_SUM, M_SSQ, M_MEAN, M_RSTD, M_MAXL, M_SCORE, M_TMP, M_TE = \
            0, 8, 16, 24, 32, 40, 48, 64

        XB = HB[:, 2048:4096]
        XF2 = HB[:, 0:2048]
        V129 = 4096
        XNT = [HB[:, 4096:6144], HB[:, 6144:8192]]
        CUM8 = KV[0:8, 0:1024]
        M8F = KV[0:8, 1024:2048]
        SROWF = KV[0:1, 1024:2048]
        T18 = XF[0:8, 0:1024]
        RKF8 = XF[0:8, 1024:2048]
        ONES8 = SREP[0:8, 0:1024]
        MROWS = QB[0:8, 0:GRP]
        MRROW = QB[0:1, 0:1024]

        def barrier():
            ev = {"P": P.op("pe", lambda E: E.drain(), inc="P"),
                  "V": P.op("dve", lambda E: E.drain(), inc="V"),
                  "A": P.op("act", lambda E: E.drain(), inc="A")}
            for k in SEMKEYS:
                if k not in ev:
                    ev[k] = P.c[k]
            own = {"pe": "P", "dve": "V", "act": "A", "gp": None, "sync": None}
            for e in ENGINES:
                P.op(e, None, waits=[(k, v) for k, v in ev.items() if k != own[e]])

        lane_i = {"W": 0, "X": 0, "S": 0, "T": 0}

        def _lane(group, lanes):
            k = lanes[lane_i[group] % len(lanes)]
            lane_i[group] += 1
            return k

        def dma_sync(dst, src, waits=()):
            k = _lane("X", XLANES)
            return (k, P.op("sync", lambda E, d=dst, s=src: E.dma_start(out=d, in_=s),
                            waits=waits, inc=k, dma=True))

        def dma_store(dst, src, waits=()):
            k = _lane("S", SLANES)
            return (k, P.op("sync", lambda E, d=dst, s=src: E.dma_start(out=d, in_=s),
                            waits=waits, inc=k, dma=True))

        def dma_w(dst, src, waits=()):
            k = _lane("W", WLANES)
            return (k, P.op("gp", lambda E, d=dst, s=src: E.dma_start(out=d, in_=s),
                            waits=waits, inc=k, dma=True))

        def mm(out, lhsT, rhs, start, stop, waits=(), inc=None):
            return P.op("pe", lambda E, o=out, l=lhsT, r=rhs, st=start, sp=stop:
                        E.matmul(o, l, r, start=st, stop=sp),
                        waits=waits, inc=inc)

        def colslice_src(t2d, c0, w):
            return t2d[:, c0:c0 + w].rearrange("(kd p) j -> p kd j", p=128)

        def wa_view(off, n, inner):
            return WA[:, off:off + n].rearrange("p (a j) -> p a j", j=inner)

        BREG = {}

        def _gp_setup(E):
            BREG["tok"] = E.alloc_register("bnd_tok")
            E.reg_mov(BREG["tok"], NTOK - 1)
            BREG["lst"] = E.alloc_register("bnd_lst")
            return E.reg_mov(BREG["lst"], 8 * ECAP - 1)
        P.op("gp", _gp_setup)
        dma_sync(M01F[:, :], m01_d[:, :])
        dma_sync(IDB[:, :], idb_d[:, :])
        dma_w(RB[:, :].rearrange("p (kd j) -> p kd j", j=8),
              rt_d[:, :].rearrange("(kd p) j -> p kd j", p=128))
        dma_w(BO[:, :], bo_d[:, :])
        dma_sync(IOTA[:, :], iota_d[:, :])
        dma_sync(EOFF[:, :], eoff_d[:, :])
        P.op("dve", lambda E: E.memset(ONEB[:, :], 1.0), inc="V")
        P.op("dve", lambda E: E.memset(EPSC[:, :], EPS), inc="V")
        P.op("dve", lambda E: E.memset(ONES8, 1.0), inc="V")
        P.op("dve", lambda E: E.memset(ONES8F[:, :], 1.0), inc="V")
        P.op("dve", lambda E: E.memset(CARRY[:, :], 0.0), inc="V")
        ev_rkb = P.op("dve", lambda E: E.memset(RKU[0:1, :], BIGIDX), inc="V")
        for ee in range(8):
            dma_sync(gl_sd[ee * ECAP:(ee + 1) * ECAP, :].rearrange("n o -> o n"),
                     RKU[0:1, :], waits=[("V", ev_rkb)])
        barrier()

        xf_cons = [[]]
        ev_rstd_hold = [None]
        ev_sct_hold = {}

        def ph_ln_stats(s, g):
            t0 = g * GRP
            ev_red = ev_sq = None
            for tc in range(NTC):
                ev_x = dma_sync(XF[:, :],
                                x_d[s, t0 + tc * 128: t0 + (tc + 1) * 128, :],
                                waits=xf_cons[0])
                ev_red = P.op("dve", lambda E, t=tc: E.reduce_sum(
                    META[:, M_SUM + t: M_SUM + t + 1], XF[:, :], axis=AX.X),
                    waits=[ev_x], inc="V")
                ev_sq = P.op("act", lambda E, t=tc: E.activation(
                    out=XF[:, :], in_=XF[:, :], func=AF.Square,
                    accum_out=META[:, M_SSQ + t: M_SSQ + t + 1]),
                    waits=[ev_x, ("V", ev_red)], inc="A")
                xf_cons[0] = [("V", ev_red), ("A", ev_sq)]
            if dump_dbg and p2 and s == 0 and g == 0:
                ev_d2 = dma_store(dbg_meta2[:, :], META[:, :],
                                  waits=[("A", ev_sq), ("V", ev_red)])
                P.op("act", None, waits=[ev_d2])
            wbat = [("A", ev_sq)]
            if dump_dbg and p2 and s == 0 and g == 0:
                wbat.append(ev_d2)
            P.selfwait("dve")
            P.op("dve", lambda E: E.tensor_single_scalar(
                META[:, M_MEAN:M_MEAN + 8], META[:, M_SUM:M_SUM + 8],
                1.0 / D, ALU.mult), waits=wbat, inc="V")
            P.op("dve", lambda E: E.tensor_single_scalar(
                META[:, M_SSQ:M_SSQ + 8], META[:, M_SSQ:M_SSQ + 8],
                1.0 / D, ALU.mult), inc="V")
            P.selfwait("dve")
            P.op("dve", lambda E: E.tensor_mul(
                META[:, M_TMP:M_TMP + 8], META[:, M_MEAN:M_MEAN + 8],
                META[:, M_MEAN:M_MEAN + 8]), inc="V")
            P.selfwait("dve")
            ev_var = P.op("dve", lambda E: E.tensor_sub(
                META[:, M_SSQ:M_SSQ + 8], META[:, M_SSQ:M_SSQ + 8],
                META[:, M_TMP:M_TMP + 8]), inc="V")
            ev_sqrt = P.op("act", lambda E: E.activation(
                out=META[:, M_TMP:M_TMP + 8], in_=META[:, M_SSQ:M_SSQ + 8],
                func=AF.Sqrt, bias=EPSC[:, 0:1]), waits=[("V", ev_var)], inc="A")
            ev_rstd = P.op("dve", lambda E: E.reciprocal(
                META[:, M_RSTD:M_RSTD + 8], META[:, M_TMP:M_TMP + 8]),
                waits=[("A", ev_sqrt)], inc="V")
            ev_rstd_hold[0] = ev_rstd

        def ph_ln_main(s, g, p2):
            t0 = g * GRP
            ev_rstd = ev_rstd_hold[0]
            if skip_sweep2:
                return None
            xnt_cons = [None, None]
            xf2_cons = [list(xf_cons[0]), []]
            ev_bank_ev = [None] * 4
            ev_tr_last = [None]
            ev_lgc = [None]
            ev_ml = None
            for tc in range(NTC):
                sl2 = tc % 2
                XFS = XF if sl2 == 0 else XF2
                w = list(xf2_cons[sl2])
                if tc == 0:
                    w.append(("V", ev_rstd))
                if sl2 == 0:
                    ev_x = dma_sync(XFS[:, :],
                                    x_d[s, t0 + tc * 128: t0 + (tc + 1) * 128, :],
                                    waits=w)
                else:
                    ev_x = dma_w(XFS[:, :],
                                 x_d[s, t0 + tc * 128: t0 + (tc + 1) * 128, :],
                                 waits=w)
                wxb = [ev_x]
                if ev_tr_last[0] is not None:
                    wxb.append(("P", ev_tr_last[0]))
                ev_xb = P.op("dve", lambda E, t=tc, X_=XFS: E.tensor_single_scalar(
                    XB, X_[:, :], META[:, M_MEAN + t: M_MEAN + t + 1],
                    ALU.subtract), waits=wxb, inc="V")
                ev_dg = P.op("dve", lambda E, t=tc: E.tensor_single_scalar(
                    DIAG[:, :], IDB[:, :], META[:, M_RSTD + t: M_RSTD + t + 1],
                    ALU.mult), inc="V")
                xf2_cons[sl2] = [("V", ev_xb)]
                if p2:
                    slot = tc % 2
                    wxn = [("V", ev_xb)]
                    if xnt_cons[slot] is not None:
                        wxn.append(xnt_cons[slot])
                    ev_xnt = P.op("dve", lambda E, t=tc, sl=slot: E.tensor_single_scalar(
                        XNT[sl], XB, META[:, M_RSTD + t: M_RSTD + t + 1], ALU.mult),
                        waits=wxn, inc="V")
                    xnt_cons[slot] = dma_sync(
                        xn_sd[s * S + t0 + tc * 128: s * S + t0 + (tc + 1) * 128, :],
                        XNT[slot], waits=[("V", ev_xnt)])
                last_mm = None
                for b in range(4):
                    for kk in range(4):
                        kd = 4 * b + kk
                        w2 = []
                        if kd == 0:
                            w2 = [("V", ev_dg)]
                        if kk == 0 and ev_bank_ev[b] is not None:
                            w2.append(("V", ev_bank_ev[b]))
                        last_mm = mm(PB[b][:, kk * 128:(kk + 1) * 128],
                                     XB[:, kd * 128:(kd + 1) * 128], DIAG[:, :],
                                     True, True, waits=w2,
                                     inc=("P" if kk == 3 else None))
                    src = PB[b][:, :].rearrange("p (k j) -> p k j", j=128)
                    dst = XN[:, :].rearrange("p (kd t) -> p kd t", t=GRP)[
                        :, 4 * b:4 * b + 4, tc * 128:(tc + 1) * 128]
                    ev_bank_ev[b] = P.op("dve", lambda E, d=dst, sr=src:
                                         E.tensor_copy(out=d, in_=sr),
                                         waits=[("P", last_mm)], inc="V")
                ev_tr_last[0] = last_mm

                if p2 and p2_extras:
                    lgmm = None
                    for kd in range(KD):
                        w3 = []
                        if kd == 0:
                            w3 = [("V", ev_bank_ev[3])]
                            if ev_lgc[0] is not None:
                                w3.append(("V", ev_lgc[0]))
                        lgmm = mm(PB[4][:, 0:8],
                                  XN[:, kd * GRP + tc * 128: kd * GRP + (tc + 1) * 128],
                                  RB[:, kd * 8:(kd + 1) * 8],
                                  kd == 0, kd == KD - 1, waits=w3,
                                  inc=("P" if kd == KD - 1 else None))
                    ev_lgc[0] = P.op("dve", lambda E: E.tensor_copy(
                        out=LG[:, :], in_=PB[4][:, 0:8]),
                        waits=[("P", lgmm)], inc="V")
                    P.selfwait("dve")
                    P.op("dve", lambda E: E.max(MX8[:, :], LG[:, :]), inc="V")
                    P.selfwait("dve")
                    P.op("dve", lambda E: E.max_index(
                        IDX8[:, :], MX8[:, :], LG[:, :]), inc="V")
                    P.selfwait("dve")
                    P.op("dve", lambda E, t=tc: E.tensor_copy(
                        out=IDXG[:, t:t + 1], in_=IDX8[:, 0:1]), inc="V")
                    ev_ml = P.op("dve", lambda E, t=tc: E.tensor_copy(
                        out=META[:, M_MAXL + t: M_MAXL + t + 1],
                        in_=MX8[:, 0:1]), inc="V")

            xf_cons[0] = xf2_cons[0] + xf2_cons[1]
            if not p2 or not p2_extras:
                return None
            tok0 = s * S + g * GRP
            ev_sig = P.op("act", lambda E: E.activation(
                out=META[:, M_SCORE:M_SCORE + 8], in_=META[:, M_MAXL:M_MAXL + 8],
                func=AF.Sigmoid), waits=[("V", ev_ml)], inc="A")
            P.selfwait("dve")
            ev_sc8 = P.op("dve", lambda E: E.tensor_copy(
                out=SC8B[:, :], in_=META[:, M_SCORE:M_SCORE + 8]),
                waits=[("A", ev_sig)], inc="V")
            P.selfwait("dve")
            ev_mev = None
            for tc in range(NTC):
                wmask = [("P", ev_mtr)] if tc > 0 else []
                ev_mv = None
                for ee in range(8):
                    ev_mv = P.op("dve", lambda E, t=tc, x=ee: E.tensor_single_scalar(
                        MASKB[:, x:x + 1], IDXG[:, t:t + 1], x, ALU.is_equal),
                        waits=(wmask if ee == 0 else []), inc="V")
                w5 = [("V", ev_mv)]
                if ev_mev is not None:
                    w5.append(("V", ev_mev))
                ev_mtr = mm(PB[5][0:8, 0:128], MASKB[:, :], IDB[:, :],
                            True, True, waits=w5, inc="P")
                ev_mev = P.op("dve", lambda E, t=tc: E.tensor_copy(
                    out=MROWS[0:8, t * 128:(t + 1) * 128], in_=PB[5][0:8, 0:128]),
                    waits=[("P", ev_mtr)], inc="V")
            prev_evac = None
            for half in range(2):
                lm = None
                for tci in range(4):
                    tc = half * 4 + tci
                    w4 = [("V", ev_sc8)] if tc == 0 else []
                    if tci == 0 and prev_evac is not None:
                        w4.append(("V", prev_evac))
                    lm = mm(PB[6][0:1, tci * 128:(tci + 1) * 128],
                            SC8B[:, tc:tc + 1], IDB[:, :], True, True,
                            waits=w4, inc=("P" if tci == 3 else None))
                prev_evac = P.op("dve", lambda E, hh=half: E.tensor_copy(
                    out=SROWF[0:1, hh * 512:(hh + 1) * 512], in_=PB[6][0:1, :]),
                    waits=[("P", lm)], inc="V")
            ev_scst = dma_sync(sc_sd[tok0:tok0 + GRP, :].rearrange("n o -> o n"), SROWF,
                               waits=[("V", prev_evac)])
            P.op("dve", lambda E: E.tensor_copy(out=M8F, in_=MROWS),
                 waits=[("V", ev_mev), ev_scst], inc="V")
            P.selfwait("dve")
            P.op("dve", lambda E: E.tensor_tensor_scan(
                CUM8, ONES8, M8F, CARRY[0:8, 0:1], ALU.mult, ALU.add), inc="V")
            P.selfwait("dve")
            P.op("dve", lambda E: E.tensor_copy(
                out=CARRY[0:8, 0:1], in_=CUM8[0:8, 1023:1024]), inc="V")
            P.op("dve", lambda E: E.scalar_tensor_tensor(
                T18, CUM8, -1.0, M8F, ALU.add, ALU.mult), inc="V")
            P.selfwait("dve")
            ev_t8 = P.op("dve", lambda E: E.scalar_tensor_tensor(
                T18, M8F, EOFF[0:8, 0:1], T18, ALU.mult, ALU.add), inc="V")
            ev_rk = None
            cmm = None
            for half in range(2):
                cmm = mm(PB[4][0:1, 0:512], ONES8F[:, :],
                         T18[:, half * 512:(half + 1) * 512], True, True,
                         waits=([("V", ev_t8)] if half == 0 else [("V", ev_rk)]),
                         inc="P")
                ev_rk = P.op("dve", lambda E, hh=half: E.tensor_copy(
                    out=RKU[0:1, hh * 512:(hh + 1) * 512], in_=PB[4][0:1, 0:512]),
                    waits=[("P", cmm)], inc="V")
            ev_rkst = dma_sync(rank_sd[0:1, tok0:tok0 + GRP], RKU[0:1, :],
                               waits=[("V", ev_rk)])
            xf_cons[0] = xf_cons[0] + [("V", ev_t8), ("P", cmm)]
            for c in range(NTC):
                par = c % 2
                gc = tok0 // 128 + c
                wrc = [ev_rkst] if c == 0 else []
                if par in ev_sct_hold:
                    wrc.append(ev_sct_hold[par])
                ev_rc = dma_w(RCOL[:, par:par + 1],
                              rank_sd[0:1, tok0 + c * 128: tok0 + (c + 1) * 128]
                              .rearrange("o p -> p o"),
                              waits=wrc)
                P.op("gp", None, waits=[ev_rc])
                wk = _lane("W", WLANES)
                ev_sct_hold[par] = (wk, P.op(
                    "gp", lambda E, pp=par, gg=gc: E.indirect_dma_start(
                        gl_sd[:, :],
                        bass.IndirectOffsetOnAxis(RCOL[:, pp:pp + 1], 0),
                        IOTA[:, gg:gg + 1], None,
                        bounds_check=BREG["lst"], oob_is_err=False),
                    inc=wk, dma=True))
            return None

        def ph_kv(s, g):
            ev_wk = {}
            ev_wv = {}
            ev_klast = {}
            ev_vlast = {}
            ev_krelu = {}
            ev_vev = {}
            ev_kvmm = {}
            ev_kvadd = {}

            def emit_kv(u):
                up = u % 2
                tc = u % NTC
                sl = u // NTC
                lastkv = None
                for icl in range(2):
                    w2 = []
                    if icl == 0:
                        w2 = [("A", ev_krelu[u]), ("V", ev_vev[u])]
                        if tc == 0 and sl >= 1:
                            w2.append(("V", ev_kvadd[sl - 1]))
                    lastkv = mm(PB[4 + icl][:, 0:129],
                                QB[:, up * 256 + icl * 128: up * 256 + (icl + 1) * 128],
                                HB[:, V129 + up * 258 + icl * 129:
                                   V129 + up * 258 + (icl + 1) * 129],
                                tc == 0, tc == NTC - 1,
                                waits=w2, inc=("P" if icl == 1 else None))
                ev_kvmm[u] = lastkv
                if tc == NTC - 1:
                    la = None
                    for icl in range(2):
                        ic = 2 * sl + icl
                        la = P.op("dve", lambda E, i_=ic, il=icl: E.tensor_add(
                            KV[:, i_ * 129:(i_ + 1) * 129],
                            KV[:, i_ * 129:(i_ + 1) * 129],
                            PB[4 + il][:, 0:129]),
                            waits=([("P", lastkv)] if icl == 0 else []), inc="V")
                    ev_kvadd[sl] = la

            for sl in range(8):
                par = sl % 2
                wts = [("P", ev_klast[sl - 2])] if sl >= 2 else []
                ev_wk[sl] = dma_w(wa_view(par * 4096, 4096, 256),
                                  colslice_src(wk_d, sl * 256, 256), waits=wts)
                wts = [("P", ev_vlast[sl - 2])] if sl >= 2 else []
                ev_wv[sl] = dma_w(wa_view(8192 + par * 4096, 4096, 256),
                                  colslice_src(wv_d, sl * 256, 256), waits=wts)
                for tc in range(NTC):
                    u = sl * NTC + tc
                    up = u % 2
                    if u >= 1:
                        emit_kv(u - 1)
                    lastk = None
                    for kd in range(KD):
                        w2 = []
                        if kd == 0:
                            w2.append(ev_wk[sl])
                            if u >= 2:
                                w2.append(("A", ev_krelu[u - 2]))
                        lastk = mm(PB[up][:, 0:256],
                                   XN[:, kd * GRP + tc * 128: kd * GRP + (tc + 1) * 128],
                                   wa_view(par * 4096, 4096, 256)[:, kd, :],
                                   kd == 0, kd == KD - 1,
                                   waits=w2, inc=("P" if kd == KD - 1 else None))
                    ev_klast[sl] = lastk
                    lastv = None
                    for kd in range(KD):
                        w2 = []
                        if kd == 0:
                            w2.append(ev_wv[sl])
                            if u >= 2:
                                w2.append(("V", ev_vev[u - 2]))
                        lastv = mm(PB[2 + up][:, 0:256],
                                   XN[:, kd * GRP + tc * 128: kd * GRP + (tc + 1) * 128],
                                   wa_view(8192 + par * 4096, 4096, 256)[:, kd, :],
                                   kd == 0, kd == KD - 1,
                                   waits=w2, inc=("P" if kd == KD - 1 else None))
                    ev_vlast[sl] = lastv
                    wre = [("P", lastk)]
                    if u >= 2:
                        wre.append(("P", ev_kvmm[u - 2]))
                    ev_krelu[u] = P.op("act", lambda E, p_=up: E.activation(
                        out=QB[:, p_ * 256:(p_ + 1) * 256], in_=PB[p_][:, 0:256],
                        func=AF.Relu), waits=wre, inc="A")
                    lastve = None
                    for icl in range(2):
                        wv2 = []
                        if icl == 0:
                            wv2 = [("P", lastv)]
                            if u >= 2:
                                wv2.append(("P", ev_kvmm[u - 2]))
                        lastve = P.op("dve", lambda E, p_=up, i_=icl: E.tensor_copy(
                            out=HB[:, V129 + p_ * 258 + i_ * 129:
                                   V129 + p_ * 258 + i_ * 129 + 128],
                            in_=PB[2 + p_][:, i_ * 128:(i_ + 1) * 128]),
                            waits=wv2, inc="V")
                    ev_vev[u] = lastve
            emit_kv(8 * NTC - 1)

        def ph_kvb():
            for ic in range(IC):
                P.op("dve", lambda E, i_=ic: E.tensor_mul(
                    KVB[:, i_ * 128:(i_ + 1) * 128],
                    KV[:, i_ * 129: i_ * 129 + 128], M01F[:, :]), inc="V")
                P.op("dve", lambda E, i_=ic: E.tensor_single_scalar(
                    TMASK[:, i_ * 128:(i_ + 1) * 128], M01F[:, :],
                    KV[:, i_ * 129 + 128: i_ * 129 + 129], ALU.mult), inc="V")

        def ph_attn(s, g, ev_acc):
            hb_guard = [None]
            for sb in range(2):
                ev_wq = {}
                ev_qmm = {}
                ev_qrelu = {}
                ev_oev = {}
                def emit_qkv(ic):
                    par = ic % 2
                    w2 = [("A", ev_qrelu[ic])]
                    if ic >= 1:
                        w2.append(("V", ev_oev[ic - 1]))
                    mm(PB[2][:, :], KVB[:, ic * 128:(ic + 1) * 128],
                       QB[:, par * 512:(par + 1) * 512], True, True, waits=w2)
                    ev_nr = mm(PB[3][:, :], TMASK[:, ic * 128:(ic + 1) * 128],
                               QB[:, par * 512:(par + 1) * 512], True, True,
                               inc="P")
                    P.op("dve", lambda E: E.tensor_single_scalar(
                        RC[:, :], PB[3][:, :], 1e-6, ALU.max),
                        waits=[("P", ev_nr)], inc="V")
                    P.selfwait("dve")
                    P.op("dve", lambda E: E.reciprocal(RR[:, :], RC[:, :]), inc="V")
                    P.selfwait("dve")
                    wo2 = []
                    if ic == 0 and hb_guard[0] is not None:
                        wo2 = [("P", hb_guard[0])]
                    ev_oev[ic] = P.op("dve", lambda E, i_=ic: E.tensor_mul(
                        HB[:, i_ * 512:(i_ + 1) * 512], PB[2][:, :], RR[:, :]),
                        waits=wo2, inc="V")

                for ic in range(IC):
                    par = ic % 2
                    wts = [("P", ev_qmm[ic - 2])] if ic >= 2 else []
                    ev_wq[ic] = dma_w(wa_view(par * 2048, 2048, 128),
                                      colslice_src(wq_d, ic * 128, 128), waits=wts)
                    lastq = None
                    for kd in range(KD):
                        w2 = []
                        if kd == 0:
                            w2.append(ev_wq[ic])
                            if ic >= 2:
                                w2.append(("A", ev_qrelu[ic - 2]))
                        lastq = mm(PB[par][:, :],
                                   wa_view(par * 2048, 2048, 128)[:, kd, :],
                                   XN[:, kd * GRP + sb * 512: kd * GRP + (sb + 1) * 512],
                                   kd == 0, kd == KD - 1,
                                   waits=w2, inc=("P" if kd == KD - 1 else None))
                    ev_qmm[ic] = lastq
                    ev_qrelu[ic] = P.op("act", lambda E, p_=par: E.activation(
                        out=QB[:, p_ * 512:(p_ + 1) * 512], in_=PB[p_][:, :],
                        func=AF.Relu), waits=[("P", lastq)], inc="A")
                    if ic >= 1:
                        emit_qkv(ic - 1)
                emit_qkv(IC - 1)
                ev_ao_mm = {}
                ev_ao_ev = {}
                for q4 in range(4):
                    if q4 == 0:
                        wts = [("P", hb_guard[0])] if hb_guard[0] is not None else []
                    else:
                        wts = [("P", ev_ao_mm[(q4 - 1, 3)])]
                    ev_wo = dma_w(wa_view(8192, 8192, 512),
                                  colslice_src(wo_d, q4 * 512, 512), waits=wts)
                    for tcl in range(4):
                        tcg = sb * 4 + tcl
                        bank = 4 + tcl % 2
                        lastao = None
                        for ic in range(IC):
                            w2 = []
                            if ic == 0:
                                w2.append(ev_wo)
                                w2.append(("V", ev_oev[IC - 1]))
                                prev = ((q4, tcl - 2) if tcl >= 2 else
                                        ((q4 - 1, tcl + 2) if q4 >= 1 else None))
                                if prev is not None:
                                    w2.append(("V", ev_ao_ev[prev]))
                            lastao = mm(PB[bank][:, :],
                                        HB[:, ic * 512 + tcl * 128: ic * 512 + (tcl + 1) * 128],
                                        wa_view(8192, 8192, 512)[:, ic, :],
                                        ic == 0, False, waits=w2)
                        lastao = mm(PB[bank][:, :], ONEB[0:1, :],
                                    BO[0:1, q4 * 512:(q4 + 1) * 512],
                                    False, True, inc="P")
                        ev_ao_mm[(q4, tcl)] = lastao
                        ev_ao_ev[(q4, tcl)] = P.op(
                            "dve", lambda E, t=tcg, q=q4, b=bank: E.tensor_add(
                                ACC[:, t * D + q * 512: t * D + (q + 1) * 512],
                                ACC[:, t * D + q * 512: t * D + (q + 1) * 512],
                                PB[b][:, :]),
                            waits=[("P", lastao), ev_acc], inc="V")
                hb_guard[0] = ev_ao_mm[(3, 3)]

        def ph_ffn(s, g):
            ev_prev_y = [None]
            ev_prev_stt = {}
            guard_g = {0: None, 1: None}
            guard_u = {0: None, 1: None}

            def gu_srcs(em, fcg):
                gsrc = (gw_d[em, :, fcg * 128:(fcg + 1) * 128] if em < 8
                        else sg_d[:, fcg * 128:(fcg + 1) * 128])
                usrc = (uw_d[em, :, fcg * 128:(fcg + 1) * 128] if em < 8
                        else su_d[:, fcg * 128:(fcg + 1) * 128])
                return (gsrc.rearrange("(kd p) j -> p kd j", p=128),
                        usrc.rearrange("(kd p) j -> p kd j", p=128))

            for em in [8]:
                for fh in range(2):
                    ev_gmm = {}
                    ev_umm = {}
                    ev_pgfree = {}
                    ev_silu = {}
                    ev_h = {}
                    ev_gw_map = {}
                    ev_uw_map = {}
                    for fc0 in range(2):
                        gpar0 = fc0 % 2
                        gs, us = gu_srcs(em, fh * 8 + fc0)
                        ev_gw_map[fc0] = dma_w(
                            wa_view(gpar0 * 2048, 2048, 128), gs,
                            waits=[("P", guard_g[gpar0])] if guard_g[gpar0] else [])
                        ev_uw_map[fc0] = dma_w(
                            wa_view(4096 + gpar0 * 2048, 2048, 128), us,
                            waits=[("P", guard_u[gpar0])] if guard_u[gpar0] else [])
                    wts = [("P", ev_prev_y[0])] if ev_prev_y[0] is not None else []
                    dsrc = (dw_d[em, fh * 1024:(fh + 1) * 1024, :] if em < 8
                            else sd_d[fh * 1024:(fh + 1) * 1024, :])
                    ev_d = dma_w(wa_view(8192, 16384, D),
                                 dsrc.rearrange("(fc p) d -> p fc d", p=128),
                                 waits=wts)
                    for fc in range(8):
                        gpar = fc % 2
                        if fc >= 2:
                            gs, us = gu_srcs(em, fh * 8 + fc)
                            ev_gw_map[fc] = dma_w(
                                wa_view(gpar * 2048, 2048, 128), gs,
                                waits=[("P", ev_gmm[(fc - 2, 1)])])
                            ev_uw_map[fc] = dma_w(
                                wa_view(4096 + gpar * 2048, 2048, 128), us,
                                waits=[("P", ev_umm[(fc - 2, 1)])])
                        ev_gw = ev_gw_map[fc]
                        ev_uw = ev_uw_map[fc]
                        for tb in range(2):
                            hu = fc * 2 + tb
                            hp = hu % 2
                            lastg = None
                            for kd in range(KD):
                                w2 = []
                                if kd == 0:
                                    w2.append(ev_gw)
                                    if hu >= 2:
                                        w2.append(ev_pgfree[hu - 2])
                                lastg = mm(PB[hp][:, :],
                                           wa_view(gpar * 2048, 2048, 128)[:, kd, :],
                                           XN[:, kd * GRP + tb * 512: kd * GRP + (tb + 1) * 512],
                                           kd == 0, kd == KD - 1,
                                           waits=w2, inc=("P" if kd == KD - 1 else None))
                            ev_gmm[(fc, tb)] = lastg
                            lastu = None
                            for kd in range(KD):
                                w2 = []
                                if kd == 0:
                                    w2.append(ev_uw)
                                    if hu >= 2:
                                        w2.append(("V", ev_h[hu - 2]))
                                lastu = mm(PB[2 + hp][:, :],
                                           wa_view(4096 + gpar * 2048, 2048, 128)[:, kd, :],
                                           XN[:, kd * GRP + tb * 512: kd * GRP + (tb + 1) * 512],
                                           kd == 0, kd == KD - 1,
                                           waits=w2, inc=("P" if kd == KD - 1 else None))
                            ev_umm[(fc, tb)] = lastu
                            if em < 8:
                                ev_t1 = P.op("dve", lambda E, p_=hp, t=tb: E.tensor_mul(
                                    RC[:, :], PB[p_][:, :], SREP[:, t * 512:(t + 1) * 512]),
                                    waits=[("P", lastg)], inc="V")
                                ev_pgfree[hu] = ("V", ev_t1)
                                ev_silu[hu] = P.op("act", lambda E: E.activation(
                                    out=RR[:, :], in_=RC[:, :], func=AF.Silu),
                                    waits=[("V", ev_t1)], inc="A")
                            else:
                                wsl = [("P", lastg)]
                                if hu >= 1:
                                    wsl.append(("V", ev_h[hu - 1]))
                                ev_silu[hu] = P.op("act", lambda E, p_=hp: E.activation(
                                    out=RR[:, :], in_=PB[p_][:, :], func=AF.Silu),
                                    waits=wsl, inc="A")
                                ev_pgfree[hu] = ("A", ev_silu[hu])
                            w2 = [("A", ev_silu[hu]), ("P", lastu)]
                            if hu == 0 and ev_prev_y[0] is not None:
                                w2.append(("P", ev_prev_y[0]))
                            ev_h[hu] = P.op("dve", lambda E, p_=hp, f=fc, t=tb:
                                            E.tensor_mul(
                                HB[:, f * 1024 + t * 512: f * 1024 + (t + 1) * 512],
                                PB[2 + p_][:, :], RR[:, :]),
                                waits=w2, inc="V")
                    guard_g[0] = ev_gmm[(6, 1)]
                    guard_g[1] = ev_gmm[(7, 1)]
                    guard_u[0] = ev_umm[(6, 1)]
                    guard_u[1] = ev_umm[(7, 1)]
                    ev_stt = {}
                    lasty = None
                    for tcg in range(NTC):
                        for q4 in range(4):
                            yu = tcg * 4 + q4
                            bank = 4 + yu % 2
                            for fc in range(8):
                                w2 = []
                                if fc == 0:
                                    w2.append(("V", ev_h[15]))
                                    w2.append(ev_d)
                                    if yu >= 2:
                                        w2.append(("V", ev_stt[yu - 2]))
                                    else:
                                        pv = ev_prev_stt.get(30 + yu)
                                        if pv is not None:
                                            w2.append(("V", pv))
                                lasty = mm(PB[bank][:, :],
                                           HB[:, fc * 1024 + tcg * 128: fc * 1024 + (tcg + 1) * 128],
                                           wa_view(8192, 16384, D)[:, fc, q4 * 512:(q4 + 1) * 512],
                                           fc == 0, fc == 7,
                                           waits=w2, inc=("P" if fc == 7 else None))
                            if em < 8:
                                ev_stt[yu] = P.op(
                                    "dve", lambda E, t=tcg, q=q4, b=bank, ee=em:
                                    E.scalar_tensor_tensor(
                                        ACC[:, t * D + q * 512: t * D + (q + 1) * 512],
                                        PB[b][:, :],
                                        META[:, M_TE + t * 8 + ee: M_TE + t * 8 + ee + 1],
                                        ACC[:, t * D + q * 512: t * D + (q + 1) * 512],
                                        ALU.mult, ALU.add),
                                    waits=[("P", lasty)], inc="V")
                            else:
                                ev_stt[yu] = P.op(
                                    "dve", lambda E, t=tcg, q=q4, b=bank: E.tensor_add(
                                        ACC[:, t * D + q * 512: t * D + (q + 1) * 512],
                                        ACC[:, t * D + q * 512: t * D + (q + 1) * 512],
                                        PB[b][:, :]),
                                    waits=[("P", lasty)], inc="V")
                    ev_prev_y[0] = lasty
                    ev_prev_stt = ev_stt


        out_flat = out_d[:, :, :].rearrange("a b d -> (a b) d")
        sc_flat = sc_sd[:, :]

        def ph_expert():
            XGT = [KVB[:, 0:2048], TMASK[:, 0:2048]]
            ev_scat = {}
            ev_prev_unit = [None]
            for ex in range(8):
                barrier()
                ev_tr_last = {}
                ev_tev = {}
                ev_sc_last = None
                for c in range(NTC):
                    par = c % 2
                    wgl = []
                    if c == 0 and ev_prev_unit[0] is not None:
                        wgl.append(("V", ev_prev_unit[0]))
                    if c in ev_scat:
                        wgl.append(ev_scat[c])
                    ev_gl = dma_w(GLS[:, c:c + 1],
                                  gl_sd[ex * ECAP + c * 128: ex * ECAP + (c + 1) * 128, :],
                                  waits=wgl)
                    P.op("gp", None, waits=[ev_gl])
                    wk = _lane("W", WLANES)
                    ev_sc_last = (wk, P.op("gp", lambda E, cc=c: E.indirect_dma_start(
                        SCOLS[:, cc:cc + 1], None,
                        sc_flat, bass.IndirectOffsetOnAxis(GLS[:, cc:cc + 1], 0),
                        bounds_check=BREG["tok"], oob_is_err=False),
                        inc=wk, dma=True))
                    wxg = []
                    if c >= 2:
                        wxg.append(("P", ev_tr_last[c - 2]))
                    wk = _lane("W", WLANES)
                    ev_xg = (wk, P.op("gp", lambda E, cc=c, pp=par: E.indirect_dma_start(
                        XGT[pp], None,
                        xn_sd[:, :], bass.IndirectOffsetOnAxis(GLS[:, cc:cc + 1], 0),
                        bounds_check=BREG["tok"], oob_is_err=False),
                        waits=wxg, inc=wk, dma=True))
                    last_mm = None
                    for b in range(4):
                        for kk in range(4):
                            kd = 4 * b + kk
                            w2 = []
                            if kd == 0:
                                w2 = [ev_xg]
                            if kk == 0 and (c - 1, b) in ev_tev:
                                w2.append(("V", ev_tev[(c - 1, b)]))
                            last_mm = mm(PB[b][:, kk * 128:(kk + 1) * 128],
                                         XGT[par][:, kd * 128:(kd + 1) * 128],
                                         IDB[:, :], True, True, waits=w2,
                                         inc=("P" if kk == 3 else None))
                        srcv = PB[b][:, :].rearrange("p (k j) -> p k j", j=128)
                        dstv = XN[:, :].rearrange("p (kd t) -> p kd t", t=GRP)[
                            :, 4 * b:4 * b + 4, c * 128:(c + 1) * 128]
                        ev_tev[(c, b)] = P.op("dve", lambda E, d=dstv, sr=srcv:
                                              E.tensor_copy(out=d, in_=sr),
                                              waits=[("P", last_mm)], inc="V")
                    ev_tr_last[c] = last_mm
                ev_xe = P.op("dve", lambda E: E.engine_nop(),
                             waits=[("V", ev_tev[(NTC - 1, 3)])], inc="V")
                P.selfwait("dve")
                ev_scb = P.op("dve", lambda E: E.tensor_copy(
                    out=SCB[:, :], in_=SCOLS[:, :]),
                    waits=[ev_sc_last], inc="V")
                prev_evac = None
                for half in range(2):
                    lm = None
                    for tci in range(4):
                        c = half * 4 + tci
                        w4 = [("V", ev_scb)] if c == 0 else []
                        if tci == 0 and prev_evac is not None:
                            w4.append(("V", prev_evac))
                        lm = mm(PB[6][0:1, tci * 128:(tci + 1) * 128],
                                SCB[:, c:c + 1], IDB[:, :], True, True,
                                waits=w4, inc=("P" if tci == 3 else None))
                    prev_evac = P.op("dve", lambda E, hh=half: E.tensor_copy(
                        out=MRROW[0:1, hh * 512:(hh + 1) * 512], in_=PB[6][0:1, :]),
                        waits=[("P", lm)], inc="V")
                ev_srep = prev_evac
                for tb in range(2):
                    bmm = mm(PB[6][:, :], ONEB[0:1, :],
                             MRROW[0:1, tb * 512:(tb + 1) * 512], True, True,
                             waits=[("V", ev_srep)], inc="P")
                    ev_srep = P.op("dve", lambda E, t=tb: E.tensor_copy(
                        out=SREP[:, t * 512:(t + 1) * 512], in_=PB[6][:, :]),
                        waits=[("P", bmm)], inc="V")
                if dump_dbg and ex == 0:
                    dma_store(dbg_srep[:, :], SREP[:, :], waits=[("V", ev_srep)])
                    dma_store(dbg_scol[:, :], SCOLS[:, :])
                    dma_store(dbg_gls[:, :], GLS[:, :])
                    dma_store(dbg_xn2[:, :], XN[:, :], waits=[("V", ev_xe)])
                    barrier()
                ev_prev_y = [None]
                ev_prev_stt = {}
                guard_g = {0: None, 1: None}
                guard_u = {0: None, 1: None}
                for fh in range(2):
                    ev_gmm = {}
                    ev_umm = {}
                    ev_pgfree = {}
                    ev_silu = {}
                    ev_h = {}
                    ev_gw_map = {}
                    ev_uw_map = {}
                    for fc0 in range(2):
                        gpar0 = fc0 % 2
                        fcg = fh * 8 + fc0
                        ev_gw_map[fc0] = dma_w(
                            wa_view(gpar0 * 2048, 2048, 128),
                            gw_d[ex, :, fcg * 128:(fcg + 1) * 128]
                            .rearrange("(kd p) j -> p kd j", p=128),
                            waits=[("P", guard_g[gpar0])] if guard_g[gpar0] else [])
                        ev_uw_map[fc0] = dma_w(
                            wa_view(4096 + gpar0 * 2048, 2048, 128),
                            uw_d[ex, :, fcg * 128:(fcg + 1) * 128]
                            .rearrange("(kd p) j -> p kd j", p=128),
                            waits=[("P", guard_u[gpar0])] if guard_u[gpar0] else [])
                    wts = [("P", ev_prev_y[0])] if ev_prev_y[0] is not None else []
                    ev_d = dma_w(wa_view(8192, 16384, D),
                                 dw_d[ex, fh * 1024:(fh + 1) * 1024, :]
                                 .rearrange("(fc p) d -> p fc d", p=128),
                                 waits=wts)
                    for fc in range(8):
                        gpar = fc % 2
                        fcg = fh * 8 + fc
                        if fc >= 2:
                            ev_gw_map[fc] = dma_w(
                                wa_view(gpar * 2048, 2048, 128),
                                gw_d[ex, :, fcg * 128:(fcg + 1) * 128]
                                .rearrange("(kd p) j -> p kd j", p=128),
                                waits=[("P", ev_gmm[(fc - 2, 1)])])
                            ev_uw_map[fc] = dma_w(
                                wa_view(4096 + gpar * 2048, 2048, 128),
                                uw_d[ex, :, fcg * 128:(fcg + 1) * 128]
                                .rearrange("(kd p) j -> p kd j", p=128),
                                waits=[("P", ev_umm[(fc - 2, 1)])])
                        for tb in range(2):
                            hu = fc * 2 + tb
                            hp = hu % 2
                            lastg = None
                            for kd in range(KD):
                                w2 = []
                                if kd == 0:
                                    w2.append(ev_gw_map[fc])
                                    if hu >= 2:
                                        w2.append(ev_pgfree[hu - 2])
                                    if fh == 0 and fc == 0 and tb == 0:
                                        w2.append(("V", ev_xe))
                                lastg = mm(PB[hp][:, :],
                                           wa_view(gpar * 2048, 2048, 128)[:, kd, :],
                                           XN[:, kd * GRP + tb * 512: kd * GRP + (tb + 1) * 512],
                                           kd == 0, kd == KD - 1,
                                           waits=w2, inc=("P" if kd == KD - 1 else None))
                            ev_gmm[(fc, tb)] = lastg
                            lastu = None
                            for kd in range(KD):
                                w2 = []
                                if kd == 0:
                                    w2.append(ev_uw_map[fc])
                                    if hu >= 2:
                                        w2.append(("V", ev_h[hu - 2]))
                                lastu = mm(PB[2 + hp][:, :],
                                           wa_view(4096 + gpar * 2048, 2048, 128)[:, kd, :],
                                           XN[:, kd * GRP + tb * 512: kd * GRP + (tb + 1) * 512],
                                           kd == 0, kd == KD - 1,
                                           waits=w2, inc=("P" if kd == KD - 1 else None))
                            ev_umm[(fc, tb)] = lastu
                            ev_t1 = P.op("dve", lambda E, p_=hp, t=tb: E.tensor_mul(
                                RC[:, :], PB[p_][:, :], SREP[:, t * 512:(t + 1) * 512]),
                                waits=[("P", lastg)], inc="V")
                            ev_pgfree[hu] = ("V", ev_t1)
                            ev_silu[hu] = P.op("act", lambda E: E.activation(
                                out=RR[:, :], in_=RC[:, :], func=AF.Silu),
                                waits=[("V", ev_t1)], inc="A")
                            w2 = [("A", ev_silu[hu]), ("P", lastu)]
                            if hu == 0 and ev_prev_y[0] is not None:
                                w2.append(("P", ev_prev_y[0]))
                            ev_h[hu] = P.op("dve", lambda E, p_=hp, f=fc, t=tb:
                                            E.tensor_mul(
                                HB[:, f * 1024 + t * 512: f * 1024 + (t + 1) * 512],
                                PB[2 + p_][:, :], RR[:, :]),
                                waits=w2, inc="V")
                    guard_g[0] = ev_gmm[(6, 1)]
                    guard_g[1] = ev_gmm[(7, 1)]
                    guard_u[0] = ev_umm[(6, 1)]
                    guard_u[1] = ev_umm[(7, 1)]
                    ev_stt = {}
                    lasty = None
                    for tcg in range(NTC):
                        for q4 in range(4):
                            yu = tcg * 4 + q4
                            bank = 4 + yu % 2
                            for fc in range(8):
                                w2 = []
                                if fc == 0:
                                    w2.append(("V", ev_h[15]))
                                    w2.append(ev_d)
                                    if yu >= 2:
                                        w2.append(("V", ev_stt[yu - 2]))
                                    else:
                                        pv = ev_prev_stt.get(30 + yu)
                                        if pv is not None:
                                            w2.append(("V", pv))
                                lasty = mm(PB[bank][:, :],
                                           HB[:, fc * 1024 + tcg * 128: fc * 1024 + (tcg + 1) * 128],
                                           wa_view(8192, 16384, D)[:, fc, q4 * 512:(q4 + 1) * 512],
                                           fc == 0, fc == 7,
                                           waits=w2, inc=("P" if fc == 7 else None))
                            if fh == 0:
                                wstt = [("P", lasty)]
                                if tcg in ev_scat:
                                    wstt.append(ev_scat[tcg])
                                ev_stt[yu] = P.op(
                                    "dve", lambda E, t=tcg, q=q4, b=bank:
                                    E.tensor_single_scalar(
                                        ACC[:, t * D + q * 512: t * D + (q + 1) * 512],
                                        PB[b][:, :], SCOLS[:, t:t + 1], ALU.mult),
                                    waits=wstt, inc="V")
                            else:
                                ev_stt[yu] = P.op(
                                    "dve", lambda E, t=tcg, q=q4, b=bank:
                                    E.scalar_tensor_tensor(
                                        ACC[:, t * D + q * 512: t * D + (q + 1) * 512],
                                        PB[b][:, :], SCOLS[:, t:t + 1],
                                        ACC[:, t * D + q * 512: t * D + (q + 1) * 512],
                                        ALU.mult, ALU.add),
                                    waits=[("P", lasty)], inc="V")
                        if fh == 1:
                            tk = _lane("T", TLANES)
                            ev_scat[tcg] = (tk, P.op(
                                "gp", lambda E, t=tcg: E.indirect_dma_start(
                                    out_flat,
                                    bass.IndirectOffsetOnAxis(GLS[:, t:t + 1], 0),
                                    ACC[:, t * D:(t + 1) * D], None,
                                    bounds_check=BREG["tok"], oob_is_err=False,
                                    compute_op=ALU.add),
                                waits=[("V", ev_stt[tcg * 4 + 3])],
                                inc=tk, dma=True))
                    ev_prev_y[0] = lasty
                    ev_prev_stt = ev_stt
                    if fh == 1:
                        ev_prev_unit[0] = ev_stt[NTC * 4 - 1]
                if dump_dbg and ex == 0:
                    barrier()
                    dma_store(dbg_hb[:, :], HB[:, :])
                    dma_store(dbg_acc[:, :], ACC[:, :])
                    barrier()

        ev_store = {}
        for s in range(n_seq):
            units = ([(g, False) for g in range(n_g)] if do_attn else []) + \
                    [(g, True) for g in range(n_g)]
            uidx = 0
            ph_ln_stats(s, units[0][0])
            if do_attn:
                P.op("dve", lambda E: E.memset(KV[:, :], 0.0), inc="V")
                for vp in range(2):
                    for icl in range(2):
                        col = V129 + vp * 258 + icl * 129 + 128
                        P.op("dve", lambda E, c=col: E.memset(HB[:, c:c + 1], 1.0),
                             inc="V")
                barrier()
                for g in range(n_g):
                    ph_ln_main(s, g, p2=False)
                    barrier()
                    ph_kv(s, g)
                    uidx += 1
                    if uidx < len(units):
                        ph_ln_stats(s, units[uidx][0])
                    barrier()
                ph_kvb()
                barrier()
            else:
                barrier()
            for g in range(n_g):
                ph_ln_main(s, g, p2=True)
                barrier()
                if dump_xn:
                    dma_store(xnd_d[s, g, :, :], XN[:, :])
                wts = []
                if (s, g - 1) in ev_store:
                    wts.append(ev_store[(s, g - 1)])
                elif (s - 1, n_g - 1) in ev_store:
                    wts.append(ev_store[(s - 1, n_g - 1)])
                ev_acc = dma_sync(
                    ACC[:, :].rearrange("p (tc d) -> p tc d", d=D),
                    x_d[s, g * GRP:(g + 1) * GRP, :]
                    .rearrange("(tc p) d -> p tc d", p=128),
                    waits=wts)
                if do_attn:
                    ph_attn(s, g, ev_acc)
                    barrier()
                if do_ffn:
                    if not do_attn:
                        P.op("dve", None, waits=[ev_acc])
                    ph_ffn(s, g)
                    uidx += 1
                    if uidx < len(units):
                        ph_ln_stats(s, units[uidx][0])
                    elif s + 1 < n_seq:
                        nxt = ([(gg, False) for gg in range(n_g)] if do_attn
                               else [(gg, True) for gg in range(n_g)])
                    barrier()
                else:
                    uidx += 1
                    if uidx < len(units):
                        ph_ln_stats(s, units[uidx][0])
                if not do_ffn and not do_attn:
                    P.op("dve", None, waits=[ev_acc])
                    barrier()
                ev_store[(s, g)] = dma_store(
                    out_d[s, g * GRP:(g + 1) * GRP, :]
                    .rearrange("(tc p) d -> p tc d", p=128),
                    ACC[:, :].rearrange("p (tc d) -> p tc d", d=D))
            barrier()
        if do_ffn and p2_extras:
            ph_expert()
            barrier()
        P.op("sync", None, waits=[(k, P.c[k]) for k in SLANES + TLANES])

        P.simulate()

        with nc.Block() as block:
            @block.tensor
            def _(E):
                P.run("pe", E, sems)

            @block.vector
            def _(E):
                P.run("dve", E, sems)

            @block.scalar
            def _(E):
                P.run("act", E, sems)

            @block.gpsimd
            def _(E):
                P.run("gp", E, sems)

            @block.sync
            def _(E):
                P.run("sync", E, sems)

    return nc


def make_consts():
    m01 = np.zeros((128, 128), dtype=np.float32)
    m01[0:64, 0:64] = 1.0
    m01[64:128, 64:128] = 1.0
    idb = np.eye(128, dtype=np.float32).astype(ml_dtypes.bfloat16)
    return m01, idb


def make_in_maps(x, Wq, Wk, Wv, Wo, bo, router, gw, uw, dw, sgw, suw, sdw):
    m01, idb = make_consts()
    shared = {
        "wq": np.ascontiguousarray(np.asarray(Wq, dtype=np.float32)),
        "wk": np.ascontiguousarray(np.asarray(Wk, dtype=np.float32)),
        "wv": np.ascontiguousarray(np.asarray(Wv, dtype=np.float32)),
        "wo": np.ascontiguousarray(np.asarray(Wo, dtype=np.float32)),
        "bo": np.ascontiguousarray(np.asarray(bo, dtype=np.float32)).reshape(1, D),
        "router": np.ascontiguousarray(np.asarray(router, dtype=np.float32)),
        "gw": np.ascontiguousarray(np.asarray(gw, dtype=np.float32)),
        "uw": np.ascontiguousarray(np.asarray(uw, dtype=np.float32)),
        "dw": np.ascontiguousarray(np.asarray(dw, dtype=np.float32)),
        "sgw": np.ascontiguousarray(np.asarray(sgw, dtype=np.float32)),
        "suw": np.ascontiguousarray(np.asarray(suw, dtype=np.float32)),
        "sdw": np.ascontiguousarray(np.asarray(sdw, dtype=np.float32)),
        "m01": m01,
        "idb": idb,
        "iota": np.ascontiguousarray(np.arange(NTOK, dtype=np.uint32).reshape(NTOK // 128, 128).T),
        "eoff": (np.arange(8, dtype=np.float32) * ECAP).reshape(8, 1),
    }
    xc = np.ascontiguousarray(np.asarray(x, dtype=np.float32))
    in_maps = []
    for c in range(8):
        m = dict(shared)
        m["x"] = xc[c * NSEQ:(c + 1) * NSEQ]
        in_maps.append(m)
    return in_maps


def kernel(L0_attn_hidden_states, L0_attn_Wq, L0_attn_Wk, L0_attn_Wv,
           L0_attn_Wo, L0_attn_bo, L0_ffn_router_w, L0_ffn_gate_w,
           L0_ffn_up_w, L0_ffn_down_w, L0_ffn_shared_gate_w,
           L0_ffn_shared_up_w, L0_ffn_shared_down_w):
    in_maps = make_in_maps(
        L0_attn_hidden_states, L0_attn_Wq, L0_attn_Wk, L0_attn_Wv, L0_attn_Wo,
        L0_attn_bo, L0_ffn_router_w, L0_ffn_gate_w, L0_ffn_up_w,
        L0_ffn_down_w, L0_ffn_shared_gate_w, L0_ffn_shared_up_w,
        L0_ffn_shared_down_w)
    nc = build_program()
    res = run_bass_kernel_spmd(nc, in_maps, core_ids=list(range(8)))
    return np.concatenate([r["out"] for r in res.results], axis=0)


# version 12 (final)
# speedup vs baseline: 1.0438x; 1.0438x over previous
import numpy as np
import ml_dtypes
from contextlib import ExitStack

import concourse.bass as bass
import concourse.mybir as mybir
from concourse.bass_utils import run_bass_kernel_spmd

F32 = mybir.dt.float32
BF16 = mybir.dt.bfloat16
U32 = mybir.dt.uint32
AF = mybir.ActivationFunctionType
ALU = mybir.AluOpType
AX = mybir.AxisListType

D = 2048
S = 2048
NSEQ = 2
GRP = 1024
NG = S // GRP
NTC = GRP // 128
KD = D // 128
IC = 16
NEXP = 9
ECAP = 1024
NTOK = NSEQ * S
BIGIDX = 1 << 20
EPS = 1e-5

ENGINES = ("pe", "dve", "act", "gp", "sync")
WLANES = ("W0", "W1", "W2", "W3", "W4", "W5")
XLANES = ("X0", "X1", "X2", "X3")
SLANES = ("S0", "S1")
TLANES = ("T0", "T1")
SEMKEYS = WLANES + XLANES + SLANES + TLANES + ("P", "V", "A")


class Plan:
    def __init__(self):
        self.ops = {e: [] for e in ENGINES}
        self.c = {k: 0 for k in SEMKEYS}

    def op(self, eng, body=None, waits=(), inc=None, dma=False):
        waits = tuple((k, v) for (k, v) in waits if v is not None and v > 0)
        amt = 16 if dma else 1
        val = None
        if inc is not None:
            self.c[inc] += amt
            val = self.c[inc]
        self.ops[eng].append((body, waits, inc, val, amt))
        return val

    def selfwait(self, eng):
        own = {"pe": "P", "dve": "V", "act": "A"}[eng]
        self.op(eng, None, waits=[(own, self.c[own])])

    def run(self, eng, E, sems):
        for body, waits, inc, val, amt in self.ops[eng]:
            for (k, v) in waits:
                E.wait_ge(sems[k], v)
            ins = body(E) if body is not None else None
            if inc is not None:
                assert ins is not None, f"inc {inc} on bodyless op ({eng})"
                ins.then_inc(sems[inc], amt)

    def simulate(self):
        idx = {e: 0 for e in ENGINES}
        cnt = {k: 0 for k in SEMKEYS}
        progress = True
        while progress:
            progress = False
            for e in ENGINES:
                while idx[e] < len(self.ops[e]):
                    body, waits, inc, val, amt = self.ops[e][idx[e]]
                    if all(cnt[k] >= v for (k, v) in waits):
                        if inc is not None:
                            cnt[inc] += amt
                        idx[e] += 1
                        progress = True
                    else:
                        break
        stuck = {}
        for e in ENGINES:
            if idx[e] < len(self.ops[e]):
                body, waits, inc, val, amt = self.ops[e][idx[e]]
                stuck[e] = (idx[e], len(self.ops[e]),
                            [(k, v, cnt[k]) for (k, v) in waits])
        if stuck:
            raise RuntimeError(f"protocol deadlock: {stuck}")
        for k in SEMKEYS:
            assert cnt[k] == self.c[k], (k, cnt[k], self.c[k])


def build_program(do_attn=True, do_ffn=True, dump_xn=False,
                  n_seq=NSEQ, n_g=NG, p2_extras=True, skip_sweep2=False,
                  dump_dbg=False):
    nc = bass.Bass("TRN2", target_bir_lowering=False, detect_race_conditions=True)

    x_d = nc.dram_tensor("x", [NSEQ, S, D], F32, kind="ExternalInput")
    wq_d = nc.dram_tensor("wq", [D, D], F32, kind="ExternalInput")
    wk_d = nc.dram_tensor("wk", [D, D], F32, kind="ExternalInput")
    wv_d = nc.dram_tensor("wv", [D, D], F32, kind="ExternalInput")
    wo_d = nc.dram_tensor("wo", [D, D], F32, kind="ExternalInput")
    bo_d = nc.dram_tensor("bo", [1, D], F32, kind="ExternalInput")
    rt_d = nc.dram_tensor("router", [D, 8], F32, kind="ExternalInput")
    gw_d = nc.dram_tensor("gw", [8, D, D], F32, kind="ExternalInput")
    uw_d = nc.dram_tensor("uw", [8, D, D], F32, kind="ExternalInput")
    dw_d = nc.dram_tensor("dw", [8, D, D], F32, kind="ExternalInput")
    sg_d = nc.dram_tensor("sgw", [D, D], F32, kind="ExternalInput")
    su_d = nc.dram_tensor("suw", [D, D], F32, kind="ExternalInput")
    sd_d = nc.dram_tensor("sdw", [D, D], F32, kind="ExternalInput")
    m01_d = nc.dram_tensor("m01", [128, 128], F32, kind="ExternalInput")
    iota_d = nc.dram_tensor("iota", [128, NTOK // 128], U32, kind="ExternalInput")
    eoff_d = nc.dram_tensor("eoff", [8, 1], F32, kind="ExternalInput")
    xn_sd = nc.dram_tensor("xnsd", [NTOK, D], BF16, kind="Internal")
    sc_sd = nc.dram_tensor("scsd", [NTOK, 1], F32, kind="Internal")
    rank_sd = nc.dram_tensor("ranksd", [1, NTOK], U32, kind="Internal")
    gl_sd = nc.dram_tensor("glsd", [8 * ECAP, 1], U32, kind="Internal")
    idb_d = nc.dram_tensor("idb", [128, 128], BF16, kind="ExternalInput")
    out_d = nc.dram_tensor("out", [NSEQ, S, D], F32, kind="ExternalOutput")
    if dump_xn:
        xnd_d = nc.dram_tensor("xnd", [NSEQ, NG, 128, KD * GRP], BF16,
                               kind="ExternalOutput")
    if dump_dbg:
        dbg_hb = nc.dram_tensor("dbg_hb", [128, 8192], BF16, kind="ExternalOutput")
        dbg_kvb = nc.dram_tensor("dbg_kvb", [128, IC * 128], BF16, kind="ExternalOutput")
        dbg_tm = nc.dram_tensor("dbg_tm", [128, IC * 128], BF16, kind="ExternalOutput")
        dbg_kv = nc.dram_tensor("dbg_kv", [128, IC * 129], F32, kind="ExternalOutput")
        dbg_qb = nc.dram_tensor("dbg_qb", [128, 1024], BF16, kind="ExternalOutput")
        dbg_acc = nc.dram_tensor("dbg_acc", [128, NTC * D], F32, kind="ExternalOutput")
        dbg_meta = nc.dram_tensor("dbg_meta", [128, 256], F32, kind="ExternalOutput")
        dbg_xn2 = nc.dram_tensor("dbg_xn2", [128, KD * GRP], BF16, kind="ExternalOutput")
        dbg_meta2 = nc.dram_tensor("dbg_meta2", [128, 256], F32, kind="ExternalOutput")
        dbg_srep = nc.dram_tensor("dbg_srep", [128, 1024], F32, kind="ExternalOutput")
        dbg_scol = nc.dram_tensor("dbg_scol", [128, 8], F32, kind="ExternalOutput")
        dbg_gls = nc.dram_tensor("dbg_gls", [128, 8], U32, kind="ExternalOutput")

    P = Plan()

    with ExitStack() as ctx:
        e_ = ctx.enter_context
        XN = e_(nc.sbuf_tensor("XN", [128, KD * GRP], BF16))
        ACC = e_(nc.sbuf_tensor("ACC", [128, NTC * D], F32))
        WA = e_(nc.sbuf_tensor("WA", [128, 24576], BF16))
        HB = e_(nc.sbuf_tensor("HB", [128, 8192], BF16))
        XF = e_(nc.sbuf_tensor("XF", [128, D], F32))
        KV = e_(nc.sbuf_tensor("KV", [128, IC * 129], F32))
        KVB = e_(nc.sbuf_tensor("KVB", [128, IC * 128], BF16))
        TMASK = e_(nc.sbuf_tensor("TMASK", [128, IC * 128], BF16))
        QB = e_(nc.sbuf_tensor("QB", [128, 1024], BF16))
        DIAG = e_(nc.sbuf_tensor("DIAG", [128, 128], BF16))
        META = e_(nc.sbuf_tensor("META", [128, 256], F32))
        RC = e_(nc.sbuf_tensor("RC", [128, 512], F32))
        RR = e_(nc.sbuf_tensor("RR", [128, 512], F32))
        SREP = e_(nc.sbuf_tensor("SREP", [128, 1024], F32))
        LG = e_(nc.sbuf_tensor("LG", [128, 8], F32))
        MX8 = e_(nc.sbuf_tensor("MX8", [128, 8], F32))
        IDX8 = e_(nc.sbuf_tensor("IDX8", [128, 8], U32))
        IDXG = e_(nc.sbuf_tensor("IDXG", [128, 8], U32))
        SC8B = e_(nc.sbuf_tensor("SC8B", [128, 8], BF16))
        M01F = e_(nc.sbuf_tensor("M01F", [128, 128], F32))
        IDB = e_(nc.sbuf_tensor("IDB", [128, 128], BF16))
        RB = e_(nc.sbuf_tensor("RB", [128, KD * 8], BF16))
        BO = e_(nc.sbuf_tensor("BO", [1, D], BF16))
        ONEB = e_(nc.sbuf_tensor("ONEB", [1, 128], BF16))
        EPSC = e_(nc.sbuf_tensor("EPSC", [128, 1], F32))
        MASKB = e_(nc.sbuf_tensor("MASKB", [128, 8], BF16))
        RKU = e_(nc.sbuf_tensor("RKU", [1, 1024], U32))
        ONES8F = e_(nc.sbuf_tensor("ONES8F", [8, 1], F32))
        IOTA = e_(nc.sbuf_tensor("IOTA", [128, NTOK // 128], U32))
        GLS = e_(nc.sbuf_tensor("GLS", [128, 8], U32))
        SCOLS = e_(nc.sbuf_tensor("SCOLS", [128, 8], F32))
        SCB = e_(nc.sbuf_tensor("SCB", [128, 8], BF16))
        RCOL = e_(nc.sbuf_tensor("RCOL", [128, 2], U32))
        CARRY = e_(nc.sbuf_tensor("CARRY", [8, 1], F32))
        EOFF = e_(nc.sbuf_tensor("EOFF", [8, 1], F32))
        PB = [e_(nc.psum_tensor(f"PB{i}", [128, 512], F32)) for i in range(8)]

        sems = {k: e_(nc.semaphore(f"sem{k}")) for k in SEMKEYS}

        M_SUM, M_SSQ, M_MEAN, M_RSTD, M_MAXL, M_SCORE, M_TMP, M_TE = \
            0, 8, 16, 24, 32, 40, 48, 64

        XB = HB[:, 2048:4096]
        XF2 = HB[:, 0:2048]
        V129 = 4096
        XNT = [HB[:, 4096:6144], HB[:, 6144:8192]]
        CUM8 = KV[0:8, 0:1024]
        M8F = KV[0:8, 1024:2048]
        SROWF = KV[0:1, 1024:2048]
        T18 = XF[0:8, 0:1024]
        RKF8 = XF[0:8, 1024:2048]
        ONES8 = SREP[0:8, 0:1024]
        MROWS = QB[0:8, 0:GRP]
        MRROW = QB[0:1, 0:1024]

        def barrier():
            ev = {"P": P.op("pe", lambda E: E.drain(), inc="P"),
                  "V": P.op("dve", lambda E: E.drain(), inc="V"),
                  "A": P.op("act", lambda E: E.drain(), inc="A")}
            for k in SEMKEYS:
                if k not in ev:
                    ev[k] = P.c[k]
            own = {"pe": "P", "dve": "V", "act": "A", "gp": None, "sync": None}
            for e in ENGINES:
                P.op(e, None, waits=[(k, v) for k, v in ev.items() if k != own[e]])

        lane_i = {"W": 0, "X": 0, "S": 0, "T": 0}

        def _lane(group, lanes):
            k = lanes[lane_i[group] % len(lanes)]
            lane_i[group] += 1
            return k

        def dma_sync(dst, src, waits=()):
            k = _lane("X", XLANES)
            return (k, P.op("sync", lambda E, d=dst, s=src: E.dma_start(out=d, in_=s),
                            waits=waits, inc=k, dma=True))

        def dma_store(dst, src, waits=()):
            k = _lane("S", SLANES)
            return (k, P.op("act", lambda E, d=dst, s=src: E.dma_start(out=d, in_=s),
                            waits=waits, inc=k, dma=True))

        def dma_acc(dst, src, waits=()):
            k = _lane("X", XLANES)
            return (k, P.op("act", lambda E, d=dst, s=src: E.dma_start(out=d, in_=s),
                            waits=waits, inc=k, dma=True))

        def dma_w(dst, src, waits=()):
            k = _lane("W", WLANES)
            return (k, P.op("gp", lambda E, d=dst, s=src: E.dma_start(out=d, in_=s),
                            waits=waits, inc=k, dma=True))

        def mm(out, lhsT, rhs, start, stop, waits=(), inc=None):
            return P.op("pe", lambda E, o=out, l=lhsT, r=rhs, st=start, sp=stop:
                        E.matmul(o, l, r, start=st, stop=sp),
                        waits=waits, inc=inc)

        def colslice_src(t2d, c0, w):
            return t2d[:, c0:c0 + w].rearrange("(kd p) j -> p kd j", p=128)

        def wa_view(off, n, inner):
            return WA[:, off:off + n].rearrange("p (a j) -> p a j", j=inner)

        BREG = {}

        def _gp_setup(E):
            BREG["tok"] = E.alloc_register("bnd_tok")
            E.reg_mov(BREG["tok"], NTOK - 1)
            BREG["lst"] = E.alloc_register("bnd_lst")
            return E.reg_mov(BREG["lst"], 8 * ECAP - 1)
        P.op("gp", _gp_setup)
        dma_sync(M01F[:, :], m01_d[:, :])
        dma_sync(IDB[:, :], idb_d[:, :])
        dma_w(RB[:, :].rearrange("p (kd j) -> p kd j", j=8),
              rt_d[:, :].rearrange("(kd p) j -> p kd j", p=128))
        dma_w(BO[:, :], bo_d[:, :])
        dma_sync(IOTA[:, :], iota_d[:, :])
        dma_sync(EOFF[:, :], eoff_d[:, :])
        P.op("dve", lambda E: E.memset(ONEB[:, :], 1.0), inc="V")
        P.op("dve", lambda E: E.memset(EPSC[:, :], EPS), inc="V")
        P.op("dve", lambda E: E.memset(ONES8, 1.0), inc="V")
        P.op("dve", lambda E: E.memset(ONES8F[:, :], 1.0), inc="V")
        P.op("dve", lambda E: E.memset(CARRY[:, :], 0.0), inc="V")
        ev_rkb = P.op("dve", lambda E: E.memset(RKU[0:1, :], BIGIDX), inc="V")
        for ee in range(8):
            dma_sync(gl_sd[ee * ECAP:(ee + 1) * ECAP, :].rearrange("n o -> o n"),
                     RKU[0:1, :], waits=[("V", ev_rkb)])
        barrier()

        xf_cons = [[]]
        ev_rstd_hold = [None]
        ev_sct_hold = {}

        def ph_ln_stats(s, g):
            t0 = g * GRP
            ev_red = ev_sq = None
            for tc in range(NTC):
                ev_x = dma_sync(XF[:, :],
                                x_d[s, t0 + tc * 128: t0 + (tc + 1) * 128, :],
                                waits=xf_cons[0])
                ev_red = P.op("dve", lambda E, t=tc: E.reduce_sum(
                    META[:, M_SUM + t: M_SUM + t + 1], XF[:, :], axis=AX.X),
                    waits=[ev_x], inc="V")
                ev_sq = P.op("act", lambda E, t=tc: E.activation(
                    out=XF[:, :], in_=XF[:, :], func=AF.Square,
                    accum_out=META[:, M_SSQ + t: M_SSQ + t + 1]),
                    waits=[ev_x, ("V", ev_red)], inc="A")
                xf_cons[0] = [("V", ev_red), ("A", ev_sq)]
            if dump_dbg and p2 and s == 0 and g == 0:
                ev_d2 = dma_store(dbg_meta2[:, :], META[:, :],
                                  waits=[("A", ev_sq), ("V", ev_red)])
                P.op("act", None, waits=[ev_d2])
            wbat = [("A", ev_sq)]
            if dump_dbg and p2 and s == 0 and g == 0:
                wbat.append(ev_d2)
            P.selfwait("dve")
            P.op("dve", lambda E: E.tensor_single_scalar(
                META[:, M_MEAN:M_MEAN + 8], META[:, M_SUM:M_SUM + 8],
                1.0 / D, ALU.mult), waits=wbat, inc="V")
            P.op("dve", lambda E: E.tensor_single_scalar(
                META[:, M_SSQ:M_SSQ + 8], META[:, M_SSQ:M_SSQ + 8],
                1.0 / D, ALU.mult), inc="V")
            P.selfwait("dve")
            P.op("dve", lambda E: E.tensor_mul(
                META[:, M_TMP:M_TMP + 8], META[:, M_MEAN:M_MEAN + 8],
                META[:, M_MEAN:M_MEAN + 8]), inc="V")
            P.selfwait("dve")
            ev_var = P.op("dve", lambda E: E.tensor_sub(
                META[:, M_SSQ:M_SSQ + 8], META[:, M_SSQ:M_SSQ + 8],
                META[:, M_TMP:M_TMP + 8]), inc="V")
            ev_sqrt = P.op("act", lambda E: E.activation(
                out=META[:, M_TMP:M_TMP + 8], in_=META[:, M_SSQ:M_SSQ + 8],
                func=AF.Sqrt, bias=EPSC[:, 0:1]), waits=[("V", ev_var)], inc="A")
            ev_rstd = P.op("dve", lambda E: E.reciprocal(
                META[:, M_RSTD:M_RSTD + 8], META[:, M_TMP:M_TMP + 8]),
                waits=[("A", ev_sqrt)], inc="V")
            ev_rstd_hold[0] = ev_rstd

        def ph_ln_main(s, g, p2):
            t0 = g * GRP
            ev_rstd = ev_rstd_hold[0]
            if skip_sweep2:
                return None
            xnt_cons = [None, None]
            xf2_cons = [list(xf_cons[0]), []]
            ev_bank_ev = [None] * 4
            ev_tr_last = [None]
            ev_lgc = [None]
            ev_ml = None
            for tc in range(NTC):
                sl2 = tc % 2
                XFS = XF if sl2 == 0 else XF2
                w = list(xf2_cons[sl2])
                if tc == 0:
                    w.append(("V", ev_rstd))
                if sl2 == 0:
                    ev_x = dma_sync(XFS[:, :],
                                    x_d[s, t0 + tc * 128: t0 + (tc + 1) * 128, :],
                                    waits=w)
                else:
                    ev_x = dma_w(XFS[:, :],
                                 x_d[s, t0 + tc * 128: t0 + (tc + 1) * 128, :],
                                 waits=w)
                wxb = [ev_x]
                if ev_tr_last[0] is not None:
                    wxb.append(("P", ev_tr_last[0]))
                ev_xb = P.op("dve", lambda E, t=tc, X_=XFS: E.tensor_single_scalar(
                    XB, X_[:, :], META[:, M_MEAN + t: M_MEAN + t + 1],
                    ALU.subtract), waits=wxb, inc="V")
                ev_dg = P.op("dve", lambda E, t=tc: E.tensor_single_scalar(
                    DIAG[:, :], IDB[:, :], META[:, M_RSTD + t: M_RSTD + t + 1],
                    ALU.mult), inc="V")
                xf2_cons[sl2] = [("V", ev_xb)]
                if p2:
                    slot = tc % 2
                    wxn = [("V", ev_xb)]
                    if xnt_cons[slot] is not None:
                        wxn.append(xnt_cons[slot])
                    ev_xnt = P.op("dve", lambda E, t=tc, sl=slot: E.tensor_single_scalar(
                        XNT[sl], XB, META[:, M_RSTD + t: M_RSTD + t + 1], ALU.mult),
                        waits=wxn, inc="V")
                    xnt_cons[slot] = dma_sync(
                        xn_sd[s * S + t0 + tc * 128: s * S + t0 + (tc + 1) * 128, :],
                        XNT[slot], waits=[("V", ev_xnt)])
                last_mm = None
                for b in range(4):
                    for kk in range(4):
                        kd = 4 * b + kk
                        w2 = []
                        if kd == 0:
                            w2 = [("V", ev_dg)]
                        if kk == 0 and ev_bank_ev[b] is not None:
                            w2.append(("V", ev_bank_ev[b]))
                        last_mm = mm(PB[b][:, kk * 128:(kk + 1) * 128],
                                     XB[:, kd * 128:(kd + 1) * 128], DIAG[:, :],
                                     True, True, waits=w2,
                                     inc=("P" if kk == 3 else None))
                    src = PB[b][:, :].rearrange("p (k j) -> p k j", j=128)
                    dst = XN[:, :].rearrange("p (kd t) -> p kd t", t=GRP)[
                        :, 4 * b:4 * b + 4, tc * 128:(tc + 1) * 128]
                    ev_bank_ev[b] = P.op("dve", lambda E, d=dst, sr=src:
                                         E.tensor_copy(out=d, in_=sr),
                                         waits=[("P", last_mm)], inc="V")
                ev_tr_last[0] = last_mm

                if p2 and p2_extras:
                    lgmm = None
                    for kd in range(KD):
                        w3 = []
                        if kd == 0:
                            w3 = [("V", ev_bank_ev[3])]
                            if ev_lgc[0] is not None:
                                w3.append(("V", ev_lgc[0]))
                        lgmm = mm(PB[4][:, 0:8],
                                  XN[:, kd * GRP + tc * 128: kd * GRP + (tc + 1) * 128],
                                  RB[:, kd * 8:(kd + 1) * 8],
                                  kd == 0, kd == KD - 1, waits=w3,
                                  inc=("P" if kd == KD - 1 else None))
                    ev_lgc[0] = P.op("dve", lambda E: E.tensor_copy(
                        out=LG[:, :], in_=PB[4][:, 0:8]),
                        waits=[("P", lgmm)], inc="V")
                    P.selfwait("dve")
                    P.op("dve", lambda E: E.max(MX8[:, :], LG[:, :]), inc="V")
                    P.selfwait("dve")
                    P.op("dve", lambda E: E.max_index(
                        IDX8[:, :], MX8[:, :], LG[:, :]), inc="V")
                    P.selfwait("dve")
                    P.op("dve", lambda E, t=tc: E.tensor_copy(
                        out=IDXG[:, t:t + 1], in_=IDX8[:, 0:1]), inc="V")
                    ev_ml = P.op("dve", lambda E, t=tc: E.tensor_copy(
                        out=META[:, M_MAXL + t: M_MAXL + t + 1],
                        in_=MX8[:, 0:1]), inc="V")

            xf_cons[0] = xf2_cons[0] + xf2_cons[1]
            if not p2 or not p2_extras:
                return None
            tok0 = s * S + g * GRP
            ev_sig = P.op("act", lambda E: E.activation(
                out=META[:, M_SCORE:M_SCORE + 8], in_=META[:, M_MAXL:M_MAXL + 8],
                func=AF.Sigmoid), waits=[("V", ev_ml)], inc="A")
            P.selfwait("dve")
            ev_sc8 = P.op("dve", lambda E: E.tensor_copy(
                out=SC8B[:, :], in_=META[:, M_SCORE:M_SCORE + 8]),
                waits=[("A", ev_sig)], inc="V")
            P.selfwait("dve")
            ev_mev = None
            for tc in range(NTC):
                wmask = [("P", ev_mtr)] if tc > 0 else []
                ev_mv = None
                for ee in range(8):
                    ev_mv = P.op("dve", lambda E, t=tc, x=ee: E.tensor_single_scalar(
                        MASKB[:, x:x + 1], IDXG[:, t:t + 1], x, ALU.is_equal),
                        waits=(wmask if ee == 0 else []), inc="V")
                w5 = [("V", ev_mv)]
                if ev_mev is not None:
                    w5.append(("V", ev_mev))
                ev_mtr = mm(PB[5][0:8, 0:128], MASKB[:, :], IDB[:, :],
                            True, True, waits=w5, inc="P")
                ev_mev = P.op("dve", lambda E, t=tc: E.tensor_copy(
                    out=MROWS[0:8, t * 128:(t + 1) * 128], in_=PB[5][0:8, 0:128]),
                    waits=[("P", ev_mtr)], inc="V")
            prev_evac = None
            for half in range(2):
                lm = None
                for tci in range(4):
                    tc = half * 4 + tci
                    w4 = [("V", ev_sc8)] if tc == 0 else []
                    if tci == 0 and prev_evac is not None:
                        w4.append(("V", prev_evac))
                    lm = mm(PB[6][0:1, tci * 128:(tci + 1) * 128],
                            SC8B[:, tc:tc + 1], IDB[:, :], True, True,
                            waits=w4, inc=("P" if tci == 3 else None))
                prev_evac = P.op("dve", lambda E, hh=half: E.tensor_copy(
                    out=SROWF[0:1, hh * 512:(hh + 1) * 512], in_=PB[6][0:1, :]),
                    waits=[("P", lm)], inc="V")
            ev_scst = dma_sync(sc_sd[tok0:tok0 + GRP, :].rearrange("n o -> o n"), SROWF,
                               waits=[("V", prev_evac)])
            P.op("dve", lambda E: E.tensor_copy(out=M8F, in_=MROWS),
                 waits=[("V", ev_mev), ev_scst], inc="V")
            P.selfwait("dve")
            P.op("dve", lambda E: E.tensor_tensor_scan(
                CUM8, ONES8, M8F, CARRY[0:8, 0:1], ALU.mult, ALU.add), inc="V")
            P.selfwait("dve")
            P.op("dve", lambda E: E.tensor_copy(
                out=CARRY[0:8, 0:1], in_=CUM8[0:8, 1023:1024]), inc="V")
            P.op("dve", lambda E: E.scalar_tensor_tensor(
                T18, CUM8, -1.0, M8F, ALU.add, ALU.mult), inc="V")
            P.selfwait("dve")
            ev_t8 = P.op("dve", lambda E: E.scalar_tensor_tensor(
                T18, M8F, EOFF[0:8, 0:1], T18, ALU.mult, ALU.add), inc="V")
            ev_rk = None
            cmm = None
            for half in range(2):
                cmm = mm(PB[4][0:1, 0:512], ONES8F[:, :],
                         T18[:, half * 512:(half + 1) * 512], True, True,
                         waits=([("V", ev_t8)] if half == 0 else [("V", ev_rk)]),
                         inc="P")
                ev_rk = P.op("dve", lambda E, hh=half: E.tensor_copy(
                    out=RKU[0:1, hh * 512:(hh + 1) * 512], in_=PB[4][0:1, 0:512]),
                    waits=[("P", cmm)], inc="V")
            ev_rkst = dma_sync(rank_sd[0:1, tok0:tok0 + GRP], RKU[0:1, :],
                               waits=[("V", ev_rk)])
            xf_cons[0] = xf_cons[0] + [("V", ev_t8), ("P", cmm)]
            for c in range(NTC):
                par = c % 2
                gc = tok0 // 128 + c
                wrc = [ev_rkst] if c == 0 else []
                if par in ev_sct_hold:
                    wrc.append(ev_sct_hold[par])
                ev_rc = dma_w(RCOL[:, par:par + 1],
                              rank_sd[0:1, tok0 + c * 128: tok0 + (c + 1) * 128]
                              .rearrange("o p -> p o"),
                              waits=wrc)
                P.op("gp", None, waits=[ev_rc])
                wk = _lane("W", WLANES)
                ev_sct_hold[par] = (wk, P.op(
                    "gp", lambda E, pp=par, gg=gc: E.indirect_dma_start(
                        gl_sd[:, :],
                        bass.IndirectOffsetOnAxis(RCOL[:, pp:pp + 1], 0),
                        IOTA[:, gg:gg + 1], None,
                        bounds_check=BREG["lst"], oob_is_err=False),
                    inc=wk, dma=True))
            return None

        def ph_kv(s, g):
            ev_wk = {}
            ev_wv = {}
            ev_klast = {}
            ev_vlast = {}
            ev_krelu = {}
            ev_vev = {}
            ev_kvmm = {}
            ev_kvadd = {}

            def emit_kv(u):
                up = u % 2
                tc = u % NTC
                sl = u // NTC
                lastkv = None
                for icl in range(2):
                    w2 = []
                    if icl == 0:
                        w2 = [("A", ev_krelu[u]), ("V", ev_vev[u])]
                        if tc == 0 and sl >= 1:
                            w2.append(("V", ev_kvadd[sl - 1]))
                    lastkv = mm(PB[4 + icl][:, 0:129],
                                QB[:, up * 256 + icl * 128: up * 256 + (icl + 1) * 128],
                                HB[:, V129 + up * 258 + icl * 129:
                                   V129 + up * 258 + (icl + 1) * 129],
                                tc == 0, tc == NTC - 1,
                                waits=w2, inc=("P" if icl == 1 else None))
                ev_kvmm[u] = lastkv
                if tc == NTC - 1:
                    la = None
                    for icl in range(2):
                        ic = 2 * sl + icl
                        la = P.op("dve", lambda E, i_=ic, il=icl: E.tensor_add(
                            KV[:, i_ * 129:(i_ + 1) * 129],
                            KV[:, i_ * 129:(i_ + 1) * 129],
                            PB[4 + il][:, 0:129]),
                            waits=([("P", lastkv)] if icl == 0 else []), inc="V")
                    ev_kvadd[sl] = la

            for sl in range(8):
                par = sl % 2
                wts = [("P", ev_klast[sl - 2])] if sl >= 2 else []
                ev_wk[sl] = dma_w(wa_view(par * 4096, 4096, 256),
                                  colslice_src(wk_d, sl * 256, 256), waits=wts)
                wts = [("P", ev_vlast[sl - 2])] if sl >= 2 else []
                ev_wv[sl] = dma_w(wa_view(8192 + par * 4096, 4096, 256),
                                  colslice_src(wv_d, sl * 256, 256), waits=wts)
                for tc in range(NTC):
                    u = sl * NTC + tc
                    up = u % 2
                    if u >= 1:
                        emit_kv(u - 1)
                    lastk = None
                    for kd in range(KD):
                        w2 = []
                        if kd == 0:
                            w2.append(ev_wk[sl])
                            if u >= 2:
                                w2.append(("A", ev_krelu[u - 2]))
                        lastk = mm(PB[up][:, 0:256],
                                   XN[:, kd * GRP + tc * 128: kd * GRP + (tc + 1) * 128],
                                   wa_view(par * 4096, 4096, 256)[:, kd, :],
                                   kd == 0, kd == KD - 1,
                                   waits=w2, inc=("P" if kd == KD - 1 else None))
                    ev_klast[sl] = lastk
                    lastv = None
                    for kd in range(KD):
                        w2 = []
                        if kd == 0:
                            w2.append(ev_wv[sl])
                            if u >= 2:
                                w2.append(("V", ev_vev[u - 2]))
                        lastv = mm(PB[2 + up][:, 0:256],
                                   XN[:, kd * GRP + tc * 128: kd * GRP + (tc + 1) * 128],
                                   wa_view(8192 + par * 4096, 4096, 256)[:, kd, :],
                                   kd == 0, kd == KD - 1,
                                   waits=w2, inc=("P" if kd == KD - 1 else None))
                    ev_vlast[sl] = lastv
                    wre = [("P", lastk)]
                    if u >= 2:
                        wre.append(("P", ev_kvmm[u - 2]))
                    ev_krelu[u] = P.op("act", lambda E, p_=up: E.activation(
                        out=QB[:, p_ * 256:(p_ + 1) * 256], in_=PB[p_][:, 0:256],
                        func=AF.Relu), waits=wre, inc="A")
                    lastve = None
                    for icl in range(2):
                        wv2 = []
                        if icl == 0:
                            wv2 = [("P", lastv)]
                            if u >= 2:
                                wv2.append(("P", ev_kvmm[u - 2]))
                        lastve = P.op("dve", lambda E, p_=up, i_=icl: E.tensor_copy(
                            out=HB[:, V129 + p_ * 258 + i_ * 129:
                                   V129 + p_ * 258 + i_ * 129 + 128],
                            in_=PB[2 + p_][:, i_ * 128:(i_ + 1) * 128]),
                            waits=wv2, inc="V")
                    ev_vev[u] = lastve
            emit_kv(8 * NTC - 1)

        def ph_kvb():
            for ic in range(IC):
                P.op("dve", lambda E, i_=ic: E.tensor_mul(
                    KVB[:, i_ * 128:(i_ + 1) * 128],
                    KV[:, i_ * 129: i_ * 129 + 128], M01F[:, :]), inc="V")
                P.op("dve", lambda E, i_=ic: E.tensor_single_scalar(
                    TMASK[:, i_ * 128:(i_ + 1) * 128], M01F[:, :],
                    KV[:, i_ * 129 + 128: i_ * 129 + 129], ALU.mult), inc="V")

        def ph_attn(s, g, ev_acc):
            hb_guard = [None]
            for sb in range(2):
                ev_wq = {}
                ev_qmm = {}
                ev_qrelu = {}
                ev_oev = {}
                def emit_qkv(ic):
                    par = ic % 2
                    w2 = [("A", ev_qrelu[ic])]
                    if ic >= 1:
                        w2.append(("V", ev_oev[ic - 1]))
                    mm(PB[2][:, :], KVB[:, ic * 128:(ic + 1) * 128],
                       QB[:, par * 512:(par + 1) * 512], True, True, waits=w2)
                    ev_nr = mm(PB[3][:, :], TMASK[:, ic * 128:(ic + 1) * 128],
                               QB[:, par * 512:(par + 1) * 512], True, True,
                               inc="P")
                    P.op("dve", lambda E: E.tensor_single_scalar(
                        RC[:, :], PB[3][:, :], 1e-6, ALU.max),
                        waits=[("P", ev_nr)], inc="V")
                    P.selfwait("dve")
                    P.op("dve", lambda E: E.reciprocal(RR[:, :], RC[:, :]), inc="V")
                    P.selfwait("dve")
                    wo2 = []
                    if ic == 0 and hb_guard[0] is not None:
                        wo2 = [("P", hb_guard[0])]
                    ev_oev[ic] = P.op("dve", lambda E, i_=ic: E.tensor_mul(
                        HB[:, i_ * 512:(i_ + 1) * 512], PB[2][:, :], RR[:, :]),
                        waits=wo2, inc="V")

                for ic in range(IC):
                    par = ic % 2
                    wts = [("P", ev_qmm[ic - 2])] if ic >= 2 else []
                    ev_wq[ic] = dma_w(wa_view(par * 2048, 2048, 128),
                                      colslice_src(wq_d, ic * 128, 128), waits=wts)
                    lastq = None
                    for kd in range(KD):
                        w2 = []
                        if kd == 0:
                            w2.append(ev_wq[ic])
                            if ic >= 2:
                                w2.append(("A", ev_qrelu[ic - 2]))
                        lastq = mm(PB[par][:, :],
                                   wa_view(par * 2048, 2048, 128)[:, kd, :],
                                   XN[:, kd * GRP + sb * 512: kd * GRP + (sb + 1) * 512],
                                   kd == 0, kd == KD - 1,
                                   waits=w2, inc=("P" if kd == KD - 1 else None))
                    ev_qmm[ic] = lastq
                    ev_qrelu[ic] = P.op("act", lambda E, p_=par: E.activation(
                        out=QB[:, p_ * 512:(p_ + 1) * 512], in_=PB[p_][:, :],
                        func=AF.Relu), waits=[("P", lastq)], inc="A")
                    if ic >= 1:
                        emit_qkv(ic - 1)
                emit_qkv(IC - 1)
                ev_ao_mm = {}
                ev_ao_ev = {}
                for q4 in range(4):
                    if q4 == 0:
                        wts = [("P", hb_guard[0])] if hb_guard[0] is not None else []
                    else:
                        wts = [("P", ev_ao_mm[(q4 - 1, 3)])]
                    ev_wo = dma_w(wa_view(8192, 8192, 512),
                                  colslice_src(wo_d, q4 * 512, 512), waits=wts)
                    for tcl in range(4):
                        tcg = sb * 4 + tcl
                        bank = 4 + tcl % 2
                        lastao = None
                        for ic in range(IC):
                            w2 = []
                            if ic == 0:
                                w2.append(ev_wo)
                                w2.append(("V", ev_oev[IC - 1]))
                                prev = ((q4, tcl - 2) if tcl >= 2 else
                                        ((q4 - 1, tcl + 2) if q4 >= 1 else None))
                                if prev is not None:
                                    w2.append(("V", ev_ao_ev[prev]))
                            lastao = mm(PB[bank][:, :],
                                        HB[:, ic * 512 + tcl * 128: ic * 512 + (tcl + 1) * 128],
                                        wa_view(8192, 8192, 512)[:, ic, :],
                                        ic == 0, False, waits=w2)
                        lastao = mm(PB[bank][:, :], ONEB[0:1, :],
                                    BO[0:1, q4 * 512:(q4 + 1) * 512],
                                    False, True, inc="P")
                        ev_ao_mm[(q4, tcl)] = lastao
                        ev_ao_ev[(q4, tcl)] = P.op(
                            "dve", lambda E, t=tcg, q=q4, b=bank: E.tensor_add(
                                ACC[:, t * D + q * 512: t * D + (q + 1) * 512],
                                ACC[:, t * D + q * 512: t * D + (q + 1) * 512],
                                PB[b][:, :]),
                            waits=[("P", lastao), ev_acc], inc="V")
                hb_guard[0] = ev_ao_mm[(3, 3)]

        def ph_ffn(s, g):
            ev_prev_y = [None]
            ev_prev_stt = {}
            guard_g = {0: None, 1: None}
            guard_u = {0: None, 1: None}

            def gu_srcs(em, fcg):
                gsrc = (gw_d[em, :, fcg * 128:(fcg + 1) * 128] if em < 8
                        else sg_d[:, fcg * 128:(fcg + 1) * 128])
                usrc = (uw_d[em, :, fcg * 128:(fcg + 1) * 128] if em < 8
                        else su_d[:, fcg * 128:(fcg + 1) * 128])
                return (gsrc.rearrange("(kd p) j -> p kd j", p=128),
                        usrc.rearrange("(kd p) j -> p kd j", p=128))

            for em in [8]:
                for fh in range(2):
                    ev_gmm = {}
                    ev_umm = {}
                    ev_pgfree = {}
                    ev_silu = {}
                    ev_h = {}
                    ev_gw_map = {}
                    ev_uw_map = {}
                    for fc0 in range(2):
                        gpar0 = fc0 % 2
                        gs, us = gu_srcs(em, fh * 8 + fc0)
                        ev_gw_map[fc0] = dma_w(
                            wa_view(gpar0 * 2048, 2048, 128), gs,
                            waits=[("P", guard_g[gpar0])] if guard_g[gpar0] else [])
                        ev_uw_map[fc0] = dma_w(
                            wa_view(4096 + gpar0 * 2048, 2048, 128), us,
                            waits=[("P", guard_u[gpar0])] if guard_u[gpar0] else [])
                    wts = [("P", ev_prev_y[0])] if ev_prev_y[0] is not None else []
                    dsrc = (dw_d[em, fh * 1024:(fh + 1) * 1024, :] if em < 8
                            else sd_d[fh * 1024:(fh + 1) * 1024, :])
                    ev_d = dma_w(wa_view(8192, 16384, D),
                                 dsrc.rearrange("(fc p) d -> p fc d", p=128),
                                 waits=wts)
                    for fc in range(8):
                        gpar = fc % 2
                        if fc >= 2:
                            gs, us = gu_srcs(em, fh * 8 + fc)
                            ev_gw_map[fc] = dma_w(
                                wa_view(gpar * 2048, 2048, 128), gs,
                                waits=[("P", ev_gmm[(fc - 2, 1)])])
                            ev_uw_map[fc] = dma_w(
                                wa_view(4096 + gpar * 2048, 2048, 128), us,
                                waits=[("P", ev_umm[(fc - 2, 1)])])
                        ev_gw = ev_gw_map[fc]
                        ev_uw = ev_uw_map[fc]
                        for tb in range(2):
                            hu = fc * 2 + tb
                            hp = hu % 2
                            lastg = None
                            for kd in range(KD):
                                w2 = []
                                if kd == 0:
                                    w2.append(ev_gw)
                                    if hu >= 2:
                                        w2.append(ev_pgfree[hu - 2])
                                lastg = mm(PB[hp][:, :],
                                           wa_view(gpar * 2048, 2048, 128)[:, kd, :],
                                           XN[:, kd * GRP + tb * 512: kd * GRP + (tb + 1) * 512],
                                           kd == 0, kd == KD - 1,
                                           waits=w2, inc=("P" if kd == KD - 1 else None))
                            ev_gmm[(fc, tb)] = lastg
                            lastu = None
                            for kd in range(KD):
                                w2 = []
                                if kd == 0:
                                    w2.append(ev_uw)
                                    if hu >= 2:
                                        w2.append(("V", ev_h[hu - 2]))
                                lastu = mm(PB[2 + hp][:, :],
                                           wa_view(4096 + gpar * 2048, 2048, 128)[:, kd, :],
                                           XN[:, kd * GRP + tb * 512: kd * GRP + (tb + 1) * 512],
                                           kd == 0, kd == KD - 1,
                                           waits=w2, inc=("P" if kd == KD - 1 else None))
                            ev_umm[(fc, tb)] = lastu
                            if em < 8:
                                ev_t1 = P.op("dve", lambda E, p_=hp, t=tb: E.tensor_mul(
                                    RC[:, :], PB[p_][:, :], SREP[:, t * 512:(t + 1) * 512]),
                                    waits=[("P", lastg)], inc="V")
                                ev_pgfree[hu] = ("V", ev_t1)
                                ev_silu[hu] = P.op("act", lambda E: E.activation(
                                    out=RR[:, :], in_=RC[:, :], func=AF.Silu),
                                    waits=[("V", ev_t1)], inc="A")
                            else:
                                wsl = [("P", lastg)]
                                if hu >= 1:
                                    wsl.append(("V", ev_h[hu - 1]))
                                ev_silu[hu] = P.op("act", lambda E, p_=hp: E.activation(
                                    out=RR[:, :], in_=PB[p_][:, :], func=AF.Silu),
                                    waits=wsl, inc="A")
                                ev_pgfree[hu] = ("A", ev_silu[hu])
                            w2 = [("A", ev_silu[hu]), ("P", lastu)]
                            if hu == 0 and ev_prev_y[0] is not None:
                                w2.append(("P", ev_prev_y[0]))
                            ev_h[hu] = P.op("dve", lambda E, p_=hp, f=fc, t=tb:
                                            E.tensor_mul(
                                HB[:, f * 1024 + t * 512: f * 1024 + (t + 1) * 512],
                                PB[2 + p_][:, :], RR[:, :]),
                                waits=w2, inc="V")
                    guard_g[0] = ev_gmm[(6, 1)]
                    guard_g[1] = ev_gmm[(7, 1)]
                    guard_u[0] = ev_umm[(6, 1)]
                    guard_u[1] = ev_umm[(7, 1)]
                    ev_stt = {}
                    lasty = None
                    for tcg in range(NTC):
                        for q4 in range(4):
                            yu = tcg * 4 + q4
                            bank = 4 + yu % 2
                            for fc in range(8):
                                w2 = []
                                if fc == 0:
                                    w2.append(("V", ev_h[14 if tcg < 4 else 15]))
                                    w2.append(ev_d)
                                    if yu >= 2:
                                        w2.append(("V", ev_stt[yu - 2]))
                                    else:
                                        pv = ev_prev_stt.get(30 + yu)
                                        if pv is not None:
                                            w2.append(("V", pv))
                                lasty = mm(PB[bank][:, :],
                                           HB[:, fc * 1024 + tcg * 128: fc * 1024 + (tcg + 1) * 128],
                                           wa_view(8192, 16384, D)[:, fc, q4 * 512:(q4 + 1) * 512],
                                           fc == 0, fc == 7,
                                           waits=w2, inc=("P" if fc == 7 else None))
                            if em < 8:
                                ev_stt[yu] = P.op(
                                    "dve", lambda E, t=tcg, q=q4, b=bank, ee=em:
                                    E.scalar_tensor_tensor(
                                        ACC[:, t * D + q * 512: t * D + (q + 1) * 512],
                                        PB[b][:, :],
                                        META[:, M_TE + t * 8 + ee: M_TE + t * 8 + ee + 1],
                                        ACC[:, t * D + q * 512: t * D + (q + 1) * 512],
                                        ALU.mult, ALU.add),
                                    waits=[("P", lasty)], inc="V")
                            else:
                                ev_stt[yu] = P.op(
                                    "dve", lambda E, t=tcg, q=q4, b=bank: E.tensor_add(
                                        ACC[:, t * D + q * 512: t * D + (q + 1) * 512],
                                        ACC[:, t * D + q * 512: t * D + (q + 1) * 512],
                                        PB[b][:, :]),
                                    waits=[("P", lasty)], inc="V")
                    ev_prev_y[0] = lasty
                    ev_prev_stt = ev_stt


        out_flat = out_d[:, :, :].rearrange("a b d -> (a b) d")
        sc_flat = sc_sd[:, :]

        def ph_expert():
            XGT = [KVB[:, 0:2048], TMASK[:, 0:2048]]
            ev_scat = {}
            ev_prev_unit = [None]
            for ex in range(8):
                barrier()
                ev_tr_last = {}
                ev_tev = {}
                ev_sc_last = None
                for c in range(NTC):
                    par = c % 2
                    wgl = []
                    if c == 0 and ev_prev_unit[0] is not None:
                        wgl.append(("V", ev_prev_unit[0]))
                    if c in ev_scat:
                        wgl.append(ev_scat[c])
                    ev_gl = dma_w(GLS[:, c:c + 1],
                                  gl_sd[ex * ECAP + c * 128: ex * ECAP + (c + 1) * 128, :],
                                  waits=wgl)
                    P.op("gp", None, waits=[ev_gl])
                    wk = _lane("W", WLANES)
                    ev_sc_last = (wk, P.op("gp", lambda E, cc=c: E.indirect_dma_start(
                        SCOLS[:, cc:cc + 1], None,
                        sc_flat, bass.IndirectOffsetOnAxis(GLS[:, cc:cc + 1], 0),
                        bounds_check=BREG["tok"], oob_is_err=False),
                        inc=wk, dma=True))
                    wxg = []
                    if c >= 2:
                        wxg.append(("P", ev_tr_last[c - 2]))
                    wk = _lane("W", WLANES)
                    ev_xg = (wk, P.op("gp", lambda E, cc=c, pp=par: E.indirect_dma_start(
                        XGT[pp], None,
                        xn_sd[:, :], bass.IndirectOffsetOnAxis(GLS[:, cc:cc + 1], 0),
                        bounds_check=BREG["tok"], oob_is_err=False),
                        waits=wxg, inc=wk, dma=True))
                    last_mm = None
                    for b in range(4):
                        for kk in range(4):
                            kd = 4 * b + kk
                            w2 = []
                            if kd == 0:
                                w2 = [ev_xg]
                            if kk == 0 and (c - 1, b) in ev_tev:
                                w2.append(("V", ev_tev[(c - 1, b)]))
                            last_mm = mm(PB[b][:, kk * 128:(kk + 1) * 128],
                                         XGT[par][:, kd * 128:(kd + 1) * 128],
                                         IDB[:, :], True, True, waits=w2,
                                         inc=("P" if kk == 3 else None))
                        srcv = PB[b][:, :].rearrange("p (k j) -> p k j", j=128)
                        dstv = XN[:, :].rearrange("p (kd t) -> p kd t", t=GRP)[
                            :, 4 * b:4 * b + 4, c * 128:(c + 1) * 128]
                        ev_tev[(c, b)] = P.op("dve", lambda E, d=dstv, sr=srcv:
                                              E.tensor_copy(out=d, in_=sr),
                                              waits=[("P", last_mm)], inc="V")
                    ev_tr_last[c] = last_mm
                ev_xe = P.op("dve", lambda E: E.engine_nop(),
                             waits=[("V", ev_tev[(NTC - 1, 3)])], inc="V")
                P.selfwait("dve")
                ev_scb = P.op("dve", lambda E: E.tensor_copy(
                    out=SCB[:, :], in_=SCOLS[:, :]),
                    waits=[ev_sc_last], inc="V")
                prev_evac = None
                for half in range(2):
                    lm = None
                    for tci in range(4):
                        c = half * 4 + tci
                        w4 = [("V", ev_scb)] if c == 0 else []
                        if tci == 0 and prev_evac is not None:
                            w4.append(("V", prev_evac))
                        lm = mm(PB[6][0:1, tci * 128:(tci + 1) * 128],
                                SCB[:, c:c + 1], IDB[:, :], True, True,
                                waits=w4, inc=("P" if tci == 3 else None))
                    prev_evac = P.op("dve", lambda E, hh=half: E.tensor_copy(
                        out=MRROW[0:1, hh * 512:(hh + 1) * 512], in_=PB[6][0:1, :]),
                        waits=[("P", lm)], inc="V")
                ev_srep = prev_evac
                for tb in range(2):
                    bmm = mm(PB[6][:, :], ONEB[0:1, :],
                             MRROW[0:1, tb * 512:(tb + 1) * 512], True, True,
                             waits=[("V", ev_srep)], inc="P")
                    ev_srep = P.op("dve", lambda E, t=tb: E.tensor_copy(
                        out=SREP[:, t * 512:(t + 1) * 512], in_=PB[6][:, :]),
                        waits=[("P", bmm)], inc="V")
                if dump_dbg and ex == 0:
                    dma_store(dbg_srep[:, :], SREP[:, :], waits=[("V", ev_srep)])
                    dma_store(dbg_scol[:, :], SCOLS[:, :])
                    dma_store(dbg_gls[:, :], GLS[:, :])
                    dma_store(dbg_xn2[:, :], XN[:, :], waits=[("V", ev_xe)])
                    barrier()
                ev_prev_y = [None]
                ev_prev_stt = {}
                guard_g = {0: None, 1: None}
                guard_u = {0: None, 1: None}
                for fh in range(2):
                    ev_gmm = {}
                    ev_umm = {}
                    ev_pgfree = {}
                    ev_silu = {}
                    ev_h = {}
                    ev_gw_map = {}
                    ev_uw_map = {}
                    for fc0 in range(2):
                        gpar0 = fc0 % 2
                        fcg = fh * 8 + fc0
                        ev_gw_map[fc0] = dma_w(
                            wa_view(gpar0 * 2048, 2048, 128),
                            gw_d[ex, :, fcg * 128:(fcg + 1) * 128]
                            .rearrange("(kd p) j -> p kd j", p=128),
                            waits=[("P", guard_g[gpar0])] if guard_g[gpar0] else [])
                        ev_uw_map[fc0] = dma_w(
                            wa_view(4096 + gpar0 * 2048, 2048, 128),
                            uw_d[ex, :, fcg * 128:(fcg + 1) * 128]
                            .rearrange("(kd p) j -> p kd j", p=128),
                            waits=[("P", guard_u[gpar0])] if guard_u[gpar0] else [])
                    wts = [("P", ev_prev_y[0])] if ev_prev_y[0] is not None else []
                    ev_d = dma_w(wa_view(8192, 16384, D),
                                 dw_d[ex, fh * 1024:(fh + 1) * 1024, :]
                                 .rearrange("(fc p) d -> p fc d", p=128),
                                 waits=wts)
                    for fc in range(8):
                        gpar = fc % 2
                        fcg = fh * 8 + fc
                        if fc >= 2:
                            ev_gw_map[fc] = dma_w(
                                wa_view(gpar * 2048, 2048, 128),
                                gw_d[ex, :, fcg * 128:(fcg + 1) * 128]
                                .rearrange("(kd p) j -> p kd j", p=128),
                                waits=[("P", ev_gmm[(fc - 2, 1)])])
                            ev_uw_map[fc] = dma_w(
                                wa_view(4096 + gpar * 2048, 2048, 128),
                                uw_d[ex, :, fcg * 128:(fcg + 1) * 128]
                                .rearrange("(kd p) j -> p kd j", p=128),
                                waits=[("P", ev_umm[(fc - 2, 1)])])
                        for tb in range(2):
                            hu = fc * 2 + tb
                            hp = hu % 2
                            lastg = None
                            for kd in range(KD):
                                w2 = []
                                if kd == 0:
                                    w2.append(ev_gw_map[fc])
                                    if hu >= 2:
                                        w2.append(ev_pgfree[hu - 2])
                                    if fh == 0 and fc == 0 and tb == 0:
                                        w2.append(("V", ev_xe))
                                lastg = mm(PB[hp][:, :],
                                           wa_view(gpar * 2048, 2048, 128)[:, kd, :],
                                           XN[:, kd * GRP + tb * 512: kd * GRP + (tb + 1) * 512],
                                           kd == 0, kd == KD - 1,
                                           waits=w2, inc=("P" if kd == KD - 1 else None))
                            ev_gmm[(fc, tb)] = lastg
                            lastu = None
                            for kd in range(KD):
                                w2 = []
                                if kd == 0:
                                    w2.append(ev_uw_map[fc])
                                    if hu >= 2:
                                        w2.append(("V", ev_h[hu - 2]))
                                lastu = mm(PB[2 + hp][:, :],
                                           wa_view(4096 + gpar * 2048, 2048, 128)[:, kd, :],
                                           XN[:, kd * GRP + tb * 512: kd * GRP + (tb + 1) * 512],
                                           kd == 0, kd == KD - 1,
                                           waits=w2, inc=("P" if kd == KD - 1 else None))
                            ev_umm[(fc, tb)] = lastu
                            ev_t1 = P.op("dve", lambda E, p_=hp, t=tb: E.tensor_mul(
                                RC[:, :], PB[p_][:, :], SREP[:, t * 512:(t + 1) * 512]),
                                waits=[("P", lastg)], inc="V")
                            ev_pgfree[hu] = ("V", ev_t1)
                            ev_silu[hu] = P.op("act", lambda E: E.activation(
                                out=RR[:, :], in_=RC[:, :], func=AF.Silu),
                                waits=[("V", ev_t1)], inc="A")
                            w2 = [("A", ev_silu[hu]), ("P", lastu)]
                            if hu == 0 and ev_prev_y[0] is not None:
                                w2.append(("P", ev_prev_y[0]))
                            ev_h[hu] = P.op("dve", lambda E, p_=hp, f=fc, t=tb:
                                            E.tensor_mul(
                                HB[:, f * 1024 + t * 512: f * 1024 + (t + 1) * 512],
                                PB[2 + p_][:, :], RR[:, :]),
                                waits=w2, inc="V")
                    guard_g[0] = ev_gmm[(6, 1)]
                    guard_g[1] = ev_gmm[(7, 1)]
                    guard_u[0] = ev_umm[(6, 1)]
                    guard_u[1] = ev_umm[(7, 1)]
                    ev_stt = {}
                    lasty = None
                    for tcg in range(NTC):
                        for q4 in range(4):
                            yu = tcg * 4 + q4
                            bank = 4 + yu % 2
                            for fc in range(8):
                                w2 = []
                                if fc == 0:
                                    w2.append(("V", ev_h[14 if tcg < 4 else 15]))
                                    w2.append(ev_d)
                                    if yu >= 2:
                                        w2.append(("V", ev_stt[yu - 2]))
                                    else:
                                        pv = ev_prev_stt.get(30 + yu)
                                        if pv is not None:
                                            w2.append(("V", pv))
                                lasty = mm(PB[bank][:, :],
                                           HB[:, fc * 1024 + tcg * 128: fc * 1024 + (tcg + 1) * 128],
                                           wa_view(8192, 16384, D)[:, fc, q4 * 512:(q4 + 1) * 512],
                                           fc == 0, fc == 7,
                                           waits=w2, inc=("P" if fc == 7 else None))
                            if fh == 0:
                                wstt = [("P", lasty)]
                                if tcg in ev_scat:
                                    wstt.append(ev_scat[tcg])
                                ev_stt[yu] = P.op(
                                    "dve", lambda E, t=tcg, q=q4, b=bank:
                                    E.tensor_single_scalar(
                                        ACC[:, t * D + q * 512: t * D + (q + 1) * 512],
                                        PB[b][:, :], SCOLS[:, t:t + 1], ALU.mult),
                                    waits=wstt, inc="V")
                            else:
                                ev_stt[yu] = P.op(
                                    "dve", lambda E, t=tcg, q=q4, b=bank:
                                    E.scalar_tensor_tensor(
                                        ACC[:, t * D + q * 512: t * D + (q + 1) * 512],
                                        PB[b][:, :], SCOLS[:, t:t + 1],
                                        ACC[:, t * D + q * 512: t * D + (q + 1) * 512],
                                        ALU.mult, ALU.add),
                                    waits=[("P", lasty)], inc="V")
                        if fh == 1:
                            tk = _lane("T", TLANES)
                            ev_scat[tcg] = (tk, P.op(
                                "gp", lambda E, t=tcg: E.indirect_dma_start(
                                    out_flat,
                                    bass.IndirectOffsetOnAxis(GLS[:, t:t + 1], 0),
                                    ACC[:, t * D:(t + 1) * D], None,
                                    bounds_check=BREG["tok"], oob_is_err=False,
                                    compute_op=ALU.add),
                                waits=[("V", ev_stt[tcg * 4 + 3])],
                                inc=tk, dma=True))
                    ev_prev_y[0] = lasty
                    ev_prev_stt = ev_stt
                    if fh == 1:
                        ev_prev_unit[0] = ev_stt[NTC * 4 - 1]
                if dump_dbg and ex == 0:
                    barrier()
                    dma_store(dbg_hb[:, :], HB[:, :])
                    dma_store(dbg_acc[:, :], ACC[:, :])
                    barrier()

        ev_store = {}
        for s in range(n_seq):
            units = ([(g, False) for g in range(n_g)] if do_attn else []) + \
                    [(g, True) for g in range(n_g)]
            uidx = 0
            ph_ln_stats(s, units[0][0])
            if do_attn:
                P.op("dve", lambda E: E.memset(KV[:, :], 0.0), inc="V")
                for vp in range(2):
                    for icl in range(2):
                        col = V129 + vp * 258 + icl * 129 + 128
                        P.op("dve", lambda E, c=col: E.memset(HB[:, c:c + 1], 1.0),
                             inc="V")
                barrier()
                for g in range(n_g):
                    ph_ln_main(s, g, p2=False)
                    barrier()
                    ph_kv(s, g)
                    uidx += 1
                    if uidx < len(units):
                        ph_ln_stats(s, units[uidx][0])
                    barrier()
                ph_kvb()
                barrier()
            else:
                barrier()
            for g in range(n_g):
                ph_ln_main(s, g, p2=True)
                barrier()
                if dump_xn:
                    dma_store(xnd_d[s, g, :, :], XN[:, :])
                wts = []
                if (s, g - 1) in ev_store:
                    wts.append(ev_store[(s, g - 1)])
                elif (s - 1, n_g - 1) in ev_store:
                    wts.append(ev_store[(s - 1, n_g - 1)])
                ev_acc = dma_acc(
                    ACC[:, :].rearrange("p (tc d) -> p tc d", d=D),
                    x_d[s, g * GRP:(g + 1) * GRP, :]
                    .rearrange("(tc p) d -> p tc d", p=128),
                    waits=wts)
                if do_attn:
                    ph_attn(s, g, ev_acc)
                    barrier()
                if do_ffn:
                    if not do_attn:
                        P.op("dve", None, waits=[ev_acc])
                    ph_ffn(s, g)
                    uidx += 1
                    if uidx < len(units):
                        ph_ln_stats(s, units[uidx][0])
                    elif s + 1 < n_seq:
                        nxt = ([(gg, False) for gg in range(n_g)] if do_attn
                               else [(gg, True) for gg in range(n_g)])
                    barrier()
                else:
                    uidx += 1
                    if uidx < len(units):
                        ph_ln_stats(s, units[uidx][0])
                if not do_ffn and not do_attn:
                    P.op("dve", None, waits=[ev_acc])
                    barrier()
                ev_store[(s, g)] = dma_store(
                    out_d[s, g * GRP:(g + 1) * GRP, :]
                    .rearrange("(tc p) d -> p tc d", p=128),
                    ACC[:, :].rearrange("p (tc d) -> p tc d", d=D))
            barrier()
        if do_ffn and p2_extras:
            ph_expert()
            barrier()
        P.op("sync", None, waits=[(k, P.c[k]) for k in SLANES + TLANES])

        P.simulate()

        with nc.Block() as block:
            @block.tensor
            def _(E):
                P.run("pe", E, sems)

            @block.vector
            def _(E):
                P.run("dve", E, sems)

            @block.scalar
            def _(E):
                P.run("act", E, sems)

            @block.gpsimd
            def _(E):
                P.run("gp", E, sems)

            @block.sync
            def _(E):
                P.run("sync", E, sems)

    return nc


def make_consts():
    m01 = np.zeros((128, 128), dtype=np.float32)
    m01[0:64, 0:64] = 1.0
    m01[64:128, 64:128] = 1.0
    idb = np.eye(128, dtype=np.float32).astype(ml_dtypes.bfloat16)
    return m01, idb


def make_in_maps(x, Wq, Wk, Wv, Wo, bo, router, gw, uw, dw, sgw, suw, sdw):
    m01, idb = make_consts()
    shared = {
        "wq": np.ascontiguousarray(np.asarray(Wq, dtype=np.float32)),
        "wk": np.ascontiguousarray(np.asarray(Wk, dtype=np.float32)),
        "wv": np.ascontiguousarray(np.asarray(Wv, dtype=np.float32)),
        "wo": np.ascontiguousarray(np.asarray(Wo, dtype=np.float32)),
        "bo": np.ascontiguousarray(np.asarray(bo, dtype=np.float32)).reshape(1, D),
        "router": np.ascontiguousarray(np.asarray(router, dtype=np.float32)),
        "gw": np.ascontiguousarray(np.asarray(gw, dtype=np.float32)),
        "uw": np.ascontiguousarray(np.asarray(uw, dtype=np.float32)),
        "dw": np.ascontiguousarray(np.asarray(dw, dtype=np.float32)),
        "sgw": np.ascontiguousarray(np.asarray(sgw, dtype=np.float32)),
        "suw": np.ascontiguousarray(np.asarray(suw, dtype=np.float32)),
        "sdw": np.ascontiguousarray(np.asarray(sdw, dtype=np.float32)),
        "m01": m01,
        "idb": idb,
        "iota": np.ascontiguousarray(np.arange(NTOK, dtype=np.uint32).reshape(NTOK // 128, 128).T),
        "eoff": (np.arange(8, dtype=np.float32) * ECAP).reshape(8, 1),
    }
    xc = np.ascontiguousarray(np.asarray(x, dtype=np.float32))
    in_maps = []
    for c in range(8):
        m = dict(shared)
        m["x"] = xc[c * NSEQ:(c + 1) * NSEQ]
        in_maps.append(m)
    return in_maps


def kernel(L0_attn_hidden_states, L0_attn_Wq, L0_attn_Wk, L0_attn_Wv,
           L0_attn_Wo, L0_attn_bo, L0_ffn_router_w, L0_ffn_gate_w,
           L0_ffn_up_w, L0_ffn_down_w, L0_ffn_shared_gate_w,
           L0_ffn_shared_up_w, L0_ffn_shared_down_w):
    in_maps = make_in_maps(
        L0_attn_hidden_states, L0_attn_Wq, L0_attn_Wk, L0_attn_Wv, L0_attn_Wo,
        L0_attn_bo, L0_ffn_router_w, L0_ffn_gate_w, L0_ffn_up_w,
        L0_ffn_down_w, L0_ffn_shared_gate_w, L0_ffn_shared_up_w,
        L0_ffn_shared_down_w)
    nc = build_program()
    res = run_bass_kernel_spmd(nc, in_maps, core_ids=list(range(8)))
    return np.concatenate([r["out"] for r in res.results], axis=0)
